# version 16
# baseline (speedup 1.0000x reference)
import numpy as np
import ml_dtypes
import concourse.bass as bass
import concourse.mybir as mybir
from concourse.bass_utils import run_bass_kernel_spmd

F32 = mybir.dt.float32
BF16 = mybir.dt.bfloat16
AF = mybir.ActivationFunctionType
ALU = mybir.AluOpType
AX = mybir.AxisListType.X

NCORES = 8
T = 1024
NT = 8
D = 2048
NH = 16
HD = 128
CH = 128
NCH = T // CH

ENGS = ("sy", "ve", "sc", "te", "gp")

C_S, C_Q, C_MEAN, C_VAR, C_TMP, C_RSTD, C_NB, C_EPS, C_NHALF, C_EPS2, C_M, C_Z, C_NM, C_B2, C_V32, C_ITMP = range(16)
DC_KDEC = 0
DC_QDEC = 16
DC_BD = 32
DC_NEGS = 48


NDMASEM = 12


class Item:
    __slots__ = ("eng", "fn", "deps", "amt", "needed", "sem_val", "idx", "sem")

    def __init__(self, eng, fn, deps, amt, idx, sem):
        self.eng = eng
        self.fn = fn
        self.deps = deps
        self.amt = amt
        self.needed = False
        self.sem_val = None
        self.idx = idx
        self.sem = sem


class Prog:
    def __init__(self):
        self.items = []
        self.dma_cnt = {}
        self.dma_prev = {}
        self.solo_cnt = 0

    def op(self, eng, fn, deps=(), dma=False, solo=False):
        deps = [d for d in deps if d is not None]
        if dma:
            i = self.dma_cnt.get(eng, 0)
            self.dma_cnt[eng] = i + 1
            sem = f"{eng}d{i % NDMASEM}"
            prev = self.dma_prev.get(sem)
            if prev is not None:
                deps = deps + [prev]
            amt = 16
        elif solo:
            sem = f"solo{self.solo_cnt}"
            self.solo_cnt += 1
            amt = 1
        else:
            sem = eng
            amt = 1
        it = Item(eng, fn, deps, amt, len(self.items), sem)
        if dma:
            self.dma_prev[sem] = it
        self.items.append(it)
        return it

    def emit(self, nc):
        for it in self.items:
            if it.amt == 16 or it.sem.startswith("solo"):
                it.needed = True
            for d in it.deps:
                d.needed = True
        cnt = {}
        for it in self.items:
            if it.needed:
                cnt[it.sem] = cnt.get(it.sem, 0) + it.amt
                it.sem_val = cnt[it.sem]
        from contextlib import ExitStack
        es = ExitStack()
        all_sems = sorted(set([it.sem for it in self.items if it.needed] + list(ENGS)))
        sems = {sn: es.enter_context(nc.semaphore("sem_" + sn)) for sn in all_sems}
        by_eng = {e: [it for it in self.items if it.eng == e] for e in ENGS}

        def mk(eng_name):
            def emitter(eng):
                waited = {}
                for it in by_eng[eng_name]:
                    for d in it.deps:
                        if d.sem == "te" and eng_name == "te":
                            continue
                        if d.sem_val is not None and d.sem_val > waited.get(d.sem, 0):
                            eng.wait_ge(sems[d.sem], d.sem_val)
                            waited[d.sem] = d.sem_val
                    inst = it.fn(eng)
                    if it.needed and inst is None:
                        inst = eng.nop()
                    if it.needed:
                        inst.then_inc(sems[it.sem], it.amt)
                return None
            return emitter

        with es, nc.Block() as block:
            block.sync(mk("sy"))
            block.vector(mk("ve"))
            block.scalar(mk("sc"))
            block.tensor(mk("te"))
            block.gpsimd(mk("gp"))


class Ctx:
    pass


SB_BASE = 16544
OFF_XSB = SB_BASE + 0x00000
OFF_XT = SB_BASE + 0x10000
OFF_SCR = SB_BASE + 0x18000
OFF_WST = SB_BASE + 0x2A000
OFF_WBF = SB_BASE + 0x2C000
OFF_SPARE = SB_BASE + 0x2E000
OFF_CONST = SB_BASE + 0x30000


def alloc_sbuf(ctx):
    A = ctx.nc.alloc_sbuf_tensor_at
    ctx.xsb = A("xsb", [128, 16384], F32, offset=OFF_XSB)
    ctx.qreg = A("qreg", [128, 16384], BF16, offset=OFF_XSB)
    ctx.kreg = A("kreg", [128, 16384], BF16, offset=OFF_XSB + 0x8000)
    ctx.h8T = A("h8T", [128, 32768], BF16, offset=OFF_XSB)
    ctx.xT = A("xT", [128, 16384], BF16, offset=OFF_XT)
    ctx.vreg = A("vreg", [128, 16384], BF16, offset=OFF_SCR)
    ctx.oT = A("oT", [128, 16384], BF16, offset=OFF_SCR + 0x8000)
    ctx.bias_bc = A("bias_bc", [128, 6144], F32, offset=OFF_SCR + 0x8000)
    ctx.scrT_bf = A("scrT_bf", [128, 4096], BF16, offset=OFF_SCR + 0x10000)
    ctx.scrT_f32 = A("scrT_f32", [128, 2048], F32, offset=OFF_SCR + 0x10000)
    ctx.xn_f32 = A("xn_f32", [128, 2048], F32, offset=OFF_SCR)
    ctx.lstage = A("lstage", [128, 1024], F32, offset=OFF_SCR + 0x2000)
    ctx.lg_f32 = A("lg_f32", [128, 4096], F32, offset=OFF_SCR)
    ctx.work_f32 = A("work_f32", [128, 4096], F32, offset=OFF_SCR + 0x4000)
    ctx.m_bf = A("m_bf", [128, 4096], BF16, offset=OFF_SCR + 0x4000)
    ctx.p_bf = A("p_bf", [128, 4096], BF16, offset=OFF_SCR + 0x6000)
    ctx.xnb_bf = A("xnb_bf", [128, 2048], BF16, offset=OFF_SCR + 0x6000)
    ctx.genes_bf = A("genes_bf", [128, 8192], BF16, offset=OFF_SCR + 0x8000)
    ctx.wdu_bf = A("wdu_bf", [128, 8192], BF16, offset=OFF_SCR + 0xC000)
    ctx.exprT_bf = A("exprT_bf", [128, 512], BF16, offset=OFF_SCR + 0x11000)
    ctx.dn_f32 = A("dn_f32", [128, 2048], F32, offset=OFF_SCR + 0x2000)
    ctx.scr_x = A("scr_x", [128, 8192], F32, offset=OFF_SCR)
    ctx.rope_rot = A("rope_rot", [128, 2048], BF16, offset=OFF_SCR + 0x10000)
    ctx.rope_mn = A("rope_mn", [128, 2048], BF16, offset=OFF_SCR + 0x10000 + 0x1000)
    ctx.wst = A("wst", [128, 2048], F32, offset=OFF_WST)
    ctx.wbf = A("wbf", [128, 4096], BF16, offset=OFF_WBF)
    ctx.g_all = A("g_all", [128, 2048], F32, offset=OFF_SPARE)
    ctx.spare_bf = A("spare_bf", [128, 4096], BF16, offset=OFF_SPARE)


    off = OFF_CONST

    def C(name, n, dtype, parts=128):
        nonlocal off
        t = A(name, [parts, n], dtype, offset=off)
        off += n * (2 if dtype == BF16 else 4)
        off = (off + 31) & ~31
        return t

    ctx.ident_f32 = C("ident_f32", 128, F32)
    ctx.ident_bf = C("ident_bf", 128, BF16)
    ctx.ones_bf = C("ones_bf", 128, BF16)
    ctx.delta_bf = C("delta_bf", 128, BF16)
    ctx.trilT_bf = C("trilT_bf", 128, BF16)
    ctx.cols = C("cols", 16, F32)
    ctx.dcols = C("dcols", 64, F32)
    ctx.intra_bf = C("intra_bf", 128, BF16)
    ctx.pm8 = C("pm8", 8, F32)
    ctx.dist8 = C("dist8", 8, F32)
    ctx.scol16 = C("scol16", 16, F32)
    ctx.jcol = C("jcol", 1, F32)
    ctx.fmat = C("fmat", 128, F32)
    ctx.sd_bf = C("sd_bf", 128, BF16)
    ctx.qT_bf = C("qT_bf", 128, BF16)
    ctx.qdT_bf = C("qdT_bf", 128, BF16)
    ctx.kT_bf = C("kT_bf", 128, BF16)
    ctx.kd_bf = C("kd_bf", 128, BF16)
    ctx.s_bf = C("s_bf", 128, BF16)
    ctx.zcol = C("zcol", 1, F32)
    ctx.zcol_bf = C("zcol_bf", 4, BF16)
    ctx.zrow = C("zrow", 128, F32, parts=1)
    ctx.onecol_bf = C("onecol_bf", 1, BF16)
    ctx.den_sb = C("den_sb", 1024, F32, parts=1)
    ctx.den_bf = C("den_bf", 512, BF16, parts=1)
    ctx.rep_bf = C("rep_bf", 128, BF16)
    ctx.vals32 = C("vals32", 32, F32)
    ctx.vexp = C("vexp", 32, F32)
    ctx.rs_cols = C("rs_cols", 16, F32)
    ctx.sq_trash = C("sq_trash", 2048, BF16)
    ctx.tcol = C("tcol", 4, F32)
    ctx.sccol = C("sccol", 4, F32)
    ctx.z_all = C("z_all", 16, F32)
    ctx.qd_bf = C("qd_bf", 128, BF16)
    ctx.ucols = C("ucols", 8, F32)
    ctx.cos_bf = C("cos_bf", 256, BF16)
    ctx.sin_bf = C("sin_bf", 256, BF16)
    assert off <= 229376, off

    PS = ctx.nc.alloc_psum_tensor
    ctx.ps = [PS(f"ps{i}", [128, 512], F32) for i in range(6)]
    ctx.ps_bf2 = [PS(f"ps_bf{i}", [128, 1024], BF16) for i in range(2)]
    ctx.ps_bf = ctx.ps_bf2[0]
    ctx.scan_bf = ctx.nc.alloc_sbuf_tensor_at("scan_bf", [128, 4096], BF16, offset=OFF_SCR + 0x10000)
    ctx.bank_last = [None] * 8


def col(t, i):
    return t[:, i:i + 1]


def dma(ctx, q, out_ap, in_ap, deps=()):
    return ctx.p.op(q, lambda e, o=out_ap, i=in_ap: e.dma_start(out=o, in_=i), deps, dma=True)



def emit_consts(ctx):
    a = ctx.aps
    p = ctx.p
    tk = []
    tk.append(dma(ctx, "sy", ctx.ident_f32[:], a["c_ident"]))
    tk.append(dma(ctx, "sy", ctx.ident_bf[:], a["c_ident_bf"]))
    tk.append(dma(ctx, "sy", ctx.ones_bf[:], a["c_ones_bf"]))
    tk.append(dma(ctx, "sy", ctx.delta_bf[:], a["c_delta_bf"]))
    tk.append(dma(ctx, "sy", ctx.trilT_bf[:], a["c_trilT_bf"]))
    tk.append(dma(ctx, "sy", ctx.jcol[:], a["c_jcol"]))
    tk.append(dma(ctx, "sy", ctx.pm8[:], a["c_pmask"].to_broadcast([128, 8])))
    tk.append(dma(ctx, "sy", ctx.dist8[:], a["c_dist"].to_broadcast([128, 8])))
    m0 = p.op("ve", lambda e: e.memset(ctx.cols[:], 0.0))
    m1 = p.op("ve", lambda e: e.memset(col(ctx.cols, C_EPS), 1e-5), deps=[m0])
    m2 = p.op("ve", lambda e: e.memset(col(ctx.cols, C_NHALF), -0.5), deps=[m1])
    m3 = p.op("ve", lambda e: e.memset(col(ctx.cols, C_EPS2), float(np.finfo(np.float32).eps)), deps=[m2])
    m4 = p.op("ve", lambda e: e.memset(ctx.onecol_bf[:], 1.0), deps=[m3])
    m5 = p.op("ve", lambda e: e.memset(ctx.zrow[:], 0.0), deps=[m4])
    allc = tk + [m0, m1, m2, m3, m4, m5]
    bars = [p.op(e, lambda eng: None, deps=allc) for e in ENGS]
    ctx.c_ready = [bars[-1]]
    return ctx.c_ready


def emit_load_x(ctx, src_dram):
    toks = []
    for t in range(NT):
        toks.append(dma(ctx, "sy", ctx.xsb[:, t * 2048:(t + 1) * 2048],
                        src_dram[t * 128:(t + 1) * 128, :]))
    return toks


def emit_store_x(ctx, dst_dram, tile_deps, src=None, tile_off=lambda t: t * 2048):
    src = src if src is not None else ctx.xsb
    toks = []
    for t in range(NT):
        dep = tile_deps[t] if isinstance(tile_deps, list) else tile_deps
        toks.append(dma(ctx, "gp", dst_dram[t * 128:(t + 1) * 128, :],
                        src[:, tile_off(t):tile_off(t) + 2048],
                        deps=[dep] if not isinstance(dep, list) else dep))
    return toks


def emit_build_xT(ctx, tile_deps, src=None, src_off=lambda t, ci: t * 2048 + ci * 128,
                  dst=None, dst_off=lambda t, ci: ci * 1024 + t * 128, tiles=range(NT),
                  extra_deps=()):
    src = src if src is not None else ctx.xsb
    dst = dst if dst is not None else ctx.xT
    p = ctx.p
    toks = []
    for t in tiles:
        dep = tile_deps[t] if isinstance(tile_deps, list) else tile_deps
        dd = ([dep] if not isinstance(dep, list) else list(dep)) + list(extra_deps)
        for ci in range(16):
            bi = 4 + (ci % 2)
            bank = ctx.ps[bi]
            tr = p.op("te", lambda e, t=t, ci=ci, bank=bank: e.transpose(
                bank[:, 0:128], src[:, src_off(t, ci):src_off(t, ci) + 128], ctx.ident_f32[:]),
                deps=dd + ([ctx.bank_last[bi]] if ctx.bank_last[bi] else []))
            eng = "sc" if ci % 2 == 0 else "ve"
            if eng == "sc":
                ev = p.op("sc", lambda e, t=t, ci=ci, bank=bank: e.activation(
                    dst[:, dst_off(t, ci):dst_off(t, ci) + 128], bank[:, 0:128], AF.Identity), deps=[tr])
            else:
                ev = p.op("ve", lambda e, t=t, ci=ci, bank=bank: e.tensor_copy(
                    dst[:, dst_off(t, ci):dst_off(t, ci) + 128], bank[:, 0:128]), deps=[tr])
            ctx.bank_last[bi] = ev
            toks.append(ev)
    return toks


def emit_proj(ctx, w_ap, n_groups, nblk, ci_per_blk, lhsT_fn, evac_fn, subs=range(NT),
              g_list=None, lhsT_deps=(), banks=(0, 1, 2, 3), lhsT_deps_fn=None, cast_eng="ve"):
    p = ctx.p
    subs = list(subs)
    nb = len(banks)
    halves = [subs[i:i + nb] for i in range(0, len(subs), nb)]
    if g_list is None:
        g_list = range(n_groups)
    assert ci_per_blk == 2
    out_toks = []
    wst_free = getattr(ctx, "_wst_free2", [None, None])
    wbf_free = getattr(ctx, "_wbf_free4", [None, None, None, None])
    prev_evac = {}
    bcount = getattr(ctx, "_proj_bcount", 0)
    for g in g_list:
        for half in halves:
            mm_last = {}
            for b in range(nblk):
                ws = bcount % 2
                slot = bcount % 4
                bcount += 1
                ld = dma(ctx, "sy", ctx.wst[:, ws * 1024:(ws + 1) * 1024], w_ap[g, b],
                         deps=[wst_free[ws]] if wst_free[ws] else [])
                if cast_eng == "sc":
                    cast = p.op("sc", lambda e, slot=slot, ws=ws: e.activation(
                        ctx.wbf[:, slot * 1024:(slot + 1) * 1024],
                        ctx.wst[:, ws * 1024:(ws + 1) * 1024], AF.Identity),
                        deps=[ld] + ([wbf_free[slot]] if wbf_free[slot] else []))
                else:
                    cast = p.op(cast_eng, lambda e, slot=slot, ws=ws: e.tensor_copy(
                        ctx.wbf[:, slot * 1024:(slot + 1) * 1024],
                        ctx.wst[:, ws * 1024:(ws + 1) * 1024]),
                        deps=[ld] + ([wbf_free[slot]] if wbf_free[slot] else []))
                wst_free[ws] = cast
                last_mm_blk = None
                for sub in half:
                    bank = ctx.ps[banks[half.index(sub)]]
                    for cii in range(ci_per_blk):
                        ci = b * ci_per_blk + cii
                        start = (b == 0 and cii == 0)
                        stop = (b == nblk - 1 and cii == ci_per_blk - 1)
                        deps = [cast]
                        if start:
                            deps += list(lhsT_deps)
                            if lhsT_deps_fn is not None:
                                deps += [d for d in lhsT_deps_fn(sub) if d is not None]
                            pe_ = prev_evac.get(sub)
                            if pe_ is not None:
                                deps += [pe_]
                            bl = ctx.bank_last[banks[half.index(sub)]]
                            if bl is not None:
                                deps += [bl]
                        mm = p.op("te", lambda e, sub=sub, ci=ci, cii=cii, slot=slot, bank=bank,
                                  start=start, stop=stop: e.matmul(
                            bank[:, 0:512], lhsT_fn(ci, sub),
                            ctx.wbf[:, slot * 1024 + cii * 512: slot * 1024 + (cii + 1) * 512],
                            start=start, stop=stop), deps=deps)
                        mm_last[sub] = mm
                        last_mm_blk = mm
                wbf_free[slot] = last_mm_blk
            for sub in half:
                ev = evac_fn(g, sub, ctx.ps[banks[half.index(sub)]][:, 0:512], [mm_last[sub]])
                prev_evac[sub] = ev
                ctx.bank_last[banks[half.index(sub)]] = ev
                out_toks.append(ev)
    ctx._proj_bcount = bcount
    ctx._wst_free2 = wst_free
    ctx._wbf_free4 = wbf_free
    return out_toks


def emit_layernorm(ctx, tile_deps, x_tensor=None, tile_off=lambda t: t * 2048, tiles=range(NT)):
    x_tensor = x_tensor if x_tensor is not None else ctx.xsb
    p = ctx.p
    cc = ctx.cols
    out = []
    prev = None
    for t in tiles:
        dep = tile_deps[t] if isinstance(tile_deps, list) else tile_deps
        dd = [dep] if not isinstance(dep, list) else list(dep)
        if prev is not None:
            dd = dd + [prev]
        xa = x_tensor[:, tile_off(t): tile_off(t) + 2048]
        red = p.op("ve", lambda e, xa=xa: e.tensor_reduce(col(cc, C_S), xa, AX, ALU.add), deps=dd)
        sq = p.op("sc", lambda e, xa=xa: e.activation(ctx.sq_trash[:], xa, AF.Square,
                                                      accum_out=col(cc, C_Q)), deps=dd)
        m1 = p.op("ve", lambda e: e.tensor_scalar(col(cc, C_MEAN), col(cc, C_S), 1.0 / 2048, None, ALU.mult), deps=[red])
        v1 = p.op("ve", lambda e: e.tensor_scalar(col(cc, C_VAR), col(cc, C_Q), 1.0 / 2048, None, ALU.mult), deps=[sq])
        t1 = p.op("ve", lambda e: e.tensor_tensor(col(cc, C_TMP), col(cc, C_MEAN), col(cc, C_MEAN), ALU.mult), deps=[m1])
        v2 = p.op("ve", lambda e: e.tensor_sub(col(cc, C_VAR), col(cc, C_VAR), col(cc, C_TMP)), deps=[v1, t1])
        l1 = p.op("sc", lambda e: e.activation(col(cc, C_RSTD), col(cc, C_VAR), AF.Ln, bias=col(cc, C_EPS)), deps=[v2])
        e1 = p.op("sc", lambda e: e.activation(col(cc, C_RSTD), col(cc, C_RSTD), AF.Exp, scale=col(cc, C_NHALF)), deps=[l1])
        n1 = p.op("ve", lambda e: e.tensor_scalar(col(cc, C_NB), col(cc, C_MEAN), -1.0, None, ALU.mult), deps=[m1])
        n2 = p.op("ve", lambda e: e.tensor_tensor(col(cc, C_NB), col(cc, C_NB), col(cc, C_RSTD), ALU.mult), deps=[n1, e1])
        ap_ = p.op("sc", lambda e, xa=xa: e.activation(xa, xa, AF.Identity, bias=col(cc, C_NB),
                                                       scale=col(cc, C_RSTD)), deps=[n2, red, sq])
        prev = ap_
        out.append(ap_)
    return out



def emit_l1_ffn(ctx, x_deps, out_cb):
    p = ctx.p
    a = ctx.aps
    xt = emit_build_xT(ctx, x_deps)
    spill = []
    for t in range(NT):
        spill.append(dma(ctx, "gp", a["x_spill"][t * 128:(t + 1) * 128, :],
                        ctx.xsb[:, t * 2048:(t + 1) * 2048], deps=xt[16 * t + 14: 16 * t + 16]))
    out_toks = [None] * NT
    h8_free = None
    slot_free = [None] * 4
    wst_free = getattr(ctx, "_wst_free2", [None, None])
    wbf_free = getattr(ctx, "_wbf_free4", [None, None, None, None])
    bcount = getattr(ctx, "_proj_bcount", 0)
    for s in range(2):
        h8_last = None
        sg_tok = [None, None]
        sg_free = [None, None]
        for oc in range(64):
            for half in range(2):
                ocol = oc if half == 0 else 64 + oc
                bank = ctx.ps[4 + half]
                mm = None
                for b2 in range(2):
                    ws = bcount % 2
                    slot = bcount % 4
                    bcount += 1
                    ld = dma(ctx, "sy", ctx.wst[:, ws * 1024:(ws + 1) * 1024],
                             a["L1g"][ocol][:, b2 * 1024:(b2 + 1) * 1024],
                             deps=[wst_free[ws]] if wst_free[ws] else [])
                    cast = p.op("ve", lambda e, slot=slot, ws=ws: e.tensor_copy(
                        ctx.wbf[:, slot * 1024:(slot + 1) * 1024],
                        ctx.wst[:, ws * 1024:(ws + 1) * 1024]),
                        deps=[ld] + ([wbf_free[slot]] if wbf_free[slot] else []))
                    wst_free[ws] = cast
                    for cii in range(8):
                        ci = b2 * 8 + cii
                        deps = [cast]
                        if ci == 0:
                            deps += xt[-2:] + [spill[-1]]
                            if h8_free is not None:
                                deps += [h8_free]
                            bl = ctx.bank_last[4 + half]
                            if bl is not None:
                                deps += [bl]
                        mm = p.op("te", lambda e, ci=ci, s=s, slot=slot, cii=cii, bank=bank,
                                  first=(ci == 0), last=(ci == 15): e.matmul(
                            bank[:, 0:512],
                            ctx.wbf[:, slot * 1024 + cii * 128: slot * 1024 + (cii + 1) * 128],
                            ctx.xT[:, ci * 1024 + s * 512: ci * 1024 + (s + 1) * 512],
                            start=first, stop=last), deps=deps)
                    wbf_free[slot] = mm
                if half == 0:
                    sg = p.op("sc", lambda e, oc=oc: e.activation(
                        ctx.scrT_bf[:, (oc % 2) * 512:(oc % 2) * 512 + 512],
                        ctx.ps[4][:, 0:512], AF.Silu),
                        deps=[mm] + ([sg_free[oc % 2]] if sg_free[oc % 2] else []))
                    sg_tok[oc % 2] = sg
                    ctx.bank_last[4] = sg
                else:
                    hv = p.op("ve", lambda e, oc=oc: e.tensor_tensor(
                        ctx.h8T[:, oc * 512:(oc + 1) * 512],
                        ctx.ps[5][:, 0:512],
                        ctx.scrT_bf[:, (oc % 2) * 512:(oc % 2) * 512 + 512], ALU.mult),
                        deps=[mm, sg_tok[oc % 2]])
                    sg_free[oc % 2] = hv
                    ctx.bank_last[5] = hv
                    h8_last = hv
        xrl = []
        for tt in range(4):
            t = s * 4 + tt
            dd = [spill[t]]
            if slot_free[tt] is not None:
                dd.append(slot_free[tt])
            xrl.append(dma(ctx, "gp", ctx.scr_x[:, tt * 2048:(tt + 1) * 2048],
                           a["x_spill"][t * 128:(t + 1) * 128, :], deps=dd))
        def lhsT_dn(ic, sub):
            return ctx.h8T[:, ic * 512 + (sub % 4) * 128: ic * 512 + (sub % 4) * 128 + 128]

        def evac_dn(g, sub, psum, deps):
            tt = sub % 4
            return p.op("ve", lambda e, g=g, tt=tt, psum=psum: e.tensor_tensor(
                ctx.scr_x[:, tt * 2048 + g * 512: tt * 2048 + (g + 1) * 512], psum,
                ctx.scr_x[:, tt * 2048 + g * 512: tt * 2048 + (g + 1) * 512], ALU.add),
                deps=deps + [xrl[tt]])

        ctx._wst_free2 = wst_free
        ctx._wbf_free4 = wbf_free
        ctx._proj_bcount = bcount
        dn = emit_proj(ctx, a["L1d"], 4, 32, 2, lhsT_dn, evac_dn,
                       subs=[s * 4 + i for i in range(4)], lhsT_deps=[h8_last])
        wst_free = ctx._wst_free2
        wbf_free = ctx._wbf_free4
        bcount = ctx._proj_bcount
        h8_free = dn[-1]
        ln = emit_layernorm(ctx, [dn[-1]] * NT, x_tensor=ctx.scr_x,
                            tile_off=lambda tt: tt * 2048, tiles=range(4))
        for tt in range(4):
            t = s * 4 + tt
            out_toks[t] = out_cb(t, ln[tt])
            slot_free[tt] = out_toks[t]
    ctx._wst_free2 = wst_free
    ctx._wbf_free4 = wbf_free
    ctx._proj_bcount = bcount
    return out_toks



def emit_l0_ffn(ctx, x_deps, tile_cb=None):
    p = ctx.p
    a = ctx.aps
    cc = ctx.cols
    nwb = dma(ctx, "sy", ctx.g_all[:], a["L0nw"].to_broadcast([128, 2048]),
              deps=list(x_deps) if not isinstance(x_deps, list) else [x_deps[-1]])
    tld = dma(ctx, "sy", col(ctx.tcol, 0), a["L0temp"].to_broadcast([128, 1]), deps=[nwb])
    sld = dma(ctx, "sy", col(ctx.sccol, 0), a["L0scale"].to_broadcast([128, 1]), deps=[tld])
    t1 = p.op("ve", lambda e: e.tensor_scalar(col(ctx.tcol, 1), col(ctx.tcol, 0), 0.1, None, ALU.max), deps=[tld])
    t2 = p.op("ve", lambda e: e.reciprocal(col(ctx.tcol, 2), col(ctx.tcol, 1)), deps=[t1])
    gtoks = []
    wst_free0 = getattr(ctx, "_wst_free2", [None, None])
    prev = [sld] + [w for w in wst_free0 if w is not None]
    for quad in range(4):
        ldg = dma(ctx, "sy", ctx.wst[:], a["L0genesP"][quad], deps=prev)
        cg = p.op("ve", lambda e, quad=quad: e.tensor_copy(
            ctx.genes_bf[:, quad * 2048:(quad + 1) * 2048], ctx.wst[:]), deps=[ldg])
        prev = [cg]
        gtoks.append(cg)
    for quad in range(4):
        ldw = dma(ctx, "sy", ctx.wst[:], a["L0wduP"][quad], deps=prev)
        cw = p.op("ve", lambda e, quad=quad: e.tensor_copy(
            ctx.wdu_bf[:, quad * 2048:(quad + 1) * 2048], ctx.wst[:]), deps=[ldw])
        prev = [cw]
        gtoks.append(cw)
    ctx._wst_free2 = [prev[0], prev[0]]

    rms_toks = []
    xnt_toks = []
    prev_xnb = []
    for t in range(NT):
        dep = x_deps[t] if isinstance(x_deps, list) else x_deps
        dd = [dep] if not isinstance(dep, list) else list(dep)
        xa = ctx.xsb[:, t * 2048:(t + 1) * 2048]
        sq = p.op("sc", lambda e, xa=xa: e.activation(ctx.sq_trash[:], xa, AF.Square,
                                                      accum_out=col(cc, C_Q)), deps=dd + rms_toks[-1:])
        v1 = p.op("ve", lambda e: e.tensor_scalar(col(cc, C_VAR), col(cc, C_Q), 1.0 / 2048, None, ALU.mult), deps=[sq])
        l1 = p.op("sc", lambda e: e.activation(col(cc, C_RSTD), col(cc, C_VAR), AF.Ln, bias=col(cc, C_EPS2)), deps=[v1])
        e1 = p.op("sc", lambda e, t=t: e.activation(col(ctx.rs_cols, t), col(cc, C_RSTD), AF.Exp,
                                                    scale=col(cc, C_NHALF)), deps=[l1])
        rms_toks.append(e1)
        r2 = p.op("ve", lambda e, t=t: e.tensor_tensor(col(ctx.rs_cols, 8 + t), col(ctx.rs_cols, t),
                                                       col(ctx.sccol, 0), ALU.mult), deps=[e1, sld])
        rms_toks.append(r2)
        xb = p.op("ve", lambda e, xa=xa, t=t: e.scalar_tensor_tensor(
            ctx.xnb_bf[:], xa, col(ctx.rs_cols, t), ctx.g_all[:], ALU.mult, ALU.mult),
            deps=[e1, nwb] + (prev_xnb if prev_xnb else []))
        last_ev = None
        evs2 = []
        for ci in range(16):
            tr = p.op("te", lambda e, ci=ci: e.transpose(ctx.ps_bf[:, 0:128],
                                                         ctx.xnb_bf[:, ci * 128:(ci + 1) * 128],
                                                         ctx.ident_bf[:]),
                      deps=[xb, ctx.bank_last[7]])
            ev = p.op("sc" if ci % 2 == 0 else "ve",
                      lambda e, t=t, ci=ci: (e.activation(ctx.xT[:, ci * 1024 + t * 128: ci * 1024 + (t + 1) * 128],
                                                          ctx.ps_bf[:, 0:128], AF.Identity)
                                             if ci % 2 == 0 else
                                             e.tensor_copy(ctx.xT[:, ci * 1024 + t * 128: ci * 1024 + (t + 1) * 128],
                                                           ctx.ps_bf[:, 0:128])),
                      deps=[tr])
            ctx.bank_last[7] = ev
            evs2.append(ev)
            last_ev = ev
        prev_xnb = evs2[-2:]
        xnt_toks.append(evs2[-2:])

    stage_free = [None, None]

    def evac_lg(g, sub, psum, deps):
        slot = (g * NT + sub) % 2
        ev = p.op("sc", lambda e, slot=slot, psum=psum: e.activation(
            ctx.lstage[:, slot * 512:(slot + 1) * 512], psum, AF.Identity, scale=col(ctx.tcol, 2)),
            deps=deps + [t2] + ([stage_free[slot]] if stage_free[slot] else []))
        dm = dma(ctx, "gp", a["lg_spill"][sub * 128:(sub + 1) * 128, g * 512:(g + 1) * 512],
                 ctx.lstage[:, slot * 512:(slot + 1) * 512], deps=[ev])
        stage_free[slot] = dm
        return dm

    lg_toks = emit_proj(ctx, a["L0Wc"], 8, 8, 2,
                        lambda ci, sub: ctx.xT[:, ci * 1024 + sub * 128: ci * 1024 + sub * 128 + 128],
                        evac_lg, lhsT_deps=(),
                        lhsT_deps_fn=lambda sub: xnt_toks[sub])

    ln_out = []
    prev_tile = None
    lg_free = None
    mask_free = None
    prev_slot_user = [None] * 32
    for t in range(NT):
        xa = ctx.xsb[:, t * 2048:(t + 1) * 2048]
        rl = dma(ctx, "gp", ctx.lg_f32[:], a["lg_spill"][t * 128:(t + 1) * 128, :],
                 deps=[lg_toks[-1]] + ([lg_free] if lg_free else []))
        wk = dma(ctx, "gp", ctx.work_f32[:], a["lg_spill"][t * 128:(t + 1) * 128, :],
                 deps=[rl] + ([mask_free] if mask_free else []))
        tok = wk
        for rnd in range(4):
            mx = p.op("ve", lambda e, rnd=rnd: e.max(out=ctx.vals32[:, rnd * 8:(rnd + 1) * 8],
                                                     in_=ctx.work_f32[:]), deps=[tok])
            tok = p.op("ve", lambda e, rnd=rnd: e.match_replace(
                out=ctx.work_f32[:], in_to_replace=ctx.vals32[:, rnd * 8:(rnd + 1) * 8],
                in_values=ctx.work_f32[:], imm_value=-1e30), deps=[mx])
        v32 = p.op("ve", lambda e: e.tensor_reduce(col(cc, C_V32), ctx.vals32[:], AX, ALU.min), deps=[tok])
        mcp = p.op("ve", lambda e: e.tensor_copy(col(cc, C_M), ctx.vals32[:, 0:1]), deps=[tok])
        nm = p.op("ve", lambda e: e.tensor_scalar(col(cc, C_NM), col(cc, C_M), -1.0, None, ALU.mult), deps=[mcp])
        ze = p.op("sc", lambda e: e.activation(ctx.vexp[:], ctx.vals32[:], AF.Exp, bias=col(cc, C_NM),
                                               accum_out=col(cc, C_Z)), deps=[nm, tok])
        lz = p.op("sc", lambda e: e.activation(col(cc, C_ITMP), col(cc, C_Z), AF.Ln), deps=[ze])
        b2a = p.op("ve", lambda e: e.tensor_copy(col(cc, C_B2), col(cc, C_NM)), deps=[lz, nm])
        b2b = p.op("ve", lambda e: e.tensor_sub(col(cc, C_B2), col(cc, C_B2), col(cc, C_ITMP)), deps=[b2a, lz])
        mk_ = p.op("ve", lambda e: e.tensor_scalar(ctx.m_bf[:], ctx.lg_f32[:], col(cc, C_V32), None,
                                                   ALU.is_ge), deps=[v32, tok])
        ep = p.op("sc", lambda e: e.activation(ctx.p_bf[:], ctx.lg_f32[:], AF.Exp, bias=col(cc, C_B2)),
                  deps=[b2b, mk_])
        pp = p.op("ve", lambda e: e.tensor_tensor(ctx.p_bf[:], ctx.p_bf[:], ctx.m_bf[:], ALU.mult),
                  deps=[ep])
        lg_free = ep
        mask_free = pp
        pt_evs = []
        for gc in range(32):
            tb = ctx.ps_bf2[gc % 2]
            tbi = 6 + gc % 2
            tr = p.op("te", lambda e, gc=gc, tb=tb: e.transpose(tb[:, 0:128],
                                                                ctx.p_bf[:, gc * 128:(gc + 1) * 128],
                                                                ctx.ident_bf[:]),
                      deps=[pp, ctx.bank_last[tbi]])
            ctx.bank_last[tbi] = tr
            eng = "sc" if gc % 2 == 0 else "ve"
            if eng == "sc":
                ev = p.op("sc", lambda e, gc=gc, tb=tb: e.activation(
                    ctx.scan_bf[:, gc * 128:(gc + 1) * 128], tb[:, 0:128], AF.Identity),
                    deps=[tr, prev_slot_user[gc]])
            else:
                ev = p.op("ve", lambda e, gc=gc, tb=tb: e.tensor_copy(
                    ctx.scan_bf[:, gc * 128:(gc + 1) * 128], tb[:, 0:128]),
                    deps=[tr, prev_slot_user[gc]])
            ctx.bank_last[tbi] = ev
            pt_evs.append(ev)
        eh_mm = [None, None]
        for half in range(2):
            bl = ctx.bank_last[4 + half]
            for gc in range(32):
                dep0 = [pt_evs[gc]] + gtoks
                if gc == 0 and bl is not None:
                    dep0.append(bl)
                eh_mm[half] = p.op("te", lambda e, gc=gc, half=half: e.matmul(
                    ctx.ps[4 + half][:, 0:128],
                    ctx.genes_bf[:, gc * 256 + half * 128: gc * 256 + half * 128 + 128],
                    ctx.scan_bf[:, gc * 128:(gc + 1) * 128],
                    start=(gc == 0), stop=(gc == 31)), deps=dep0)
        ex0 = p.op("ve", lambda e: e.tensor_copy(ctx.exprT_bf[:, 0:128], ctx.ps[4][:, 0:128]), deps=[eh_mm[0]])
        ex1 = p.op("sc", lambda e: e.activation(ctx.exprT_bf[:, 128:256], ctx.ps[5][:, 0:128], AF.Identity),
                   deps=[eh_mm[1]])
        prev_slot_user = [eh_mm[1]] * 32
        ctx.bank_last[4] = ex0
        ctx.bank_last[5] = ex1
        dn_evs = []
        for g in range(4):
            bl = ctx.bank_last[g % 2]
            mm1 = p.op("te", lambda e, g=g: e.matmul(ctx.ps[g % 2][:, 0:512], ctx.exprT_bf[:, 0:128],
                                                     ctx.wdu_bf[:, g * 512:(g + 1) * 512],
                                                     start=True, stop=False),
                       deps=[ex0, ex1] + ([bl] if bl else []))
            mm2 = p.op("te", lambda e, g=g: e.matmul(ctx.ps[g % 2][:, 0:512], ctx.exprT_bf[:, 128:256],
                                                     ctx.wdu_bf[:, 2048 + g * 512: 2048 + (g + 1) * 512],
                                                     start=False, stop=True), deps=[mm1])
            dv = p.op("sc", lambda e, g=g: e.activation(ctx.dn_f32[:, g * 512:(g + 1) * 512],
                                                        ctx.ps[g % 2][:, 0:512], AF.Tanh),
                      deps=[mm2, mask_free])
            ctx.bank_last[g % 2] = dv
            dn_evs.append(dv)
        up_evs = []
        for g in range(4):
            bl = ctx.bank_last[2 + g % 2]
            mm1 = p.op("te", lambda e, g=g: e.matmul(ctx.ps[2 + g % 2][:, 0:512], ctx.exprT_bf[:, 0:128],
                                                     ctx.wdu_bf[:, 4096 + g * 512: 4096 + (g + 1) * 512],
                                                     start=True, stop=False),
                       deps=[ex0, ex1] + ([bl] if bl else []))
            mm2 = p.op("te", lambda e, g=g: e.matmul(ctx.ps[2 + g % 2][:, 0:512], ctx.exprT_bf[:, 128:256],
                                                     ctx.wdu_bf[:, 6144 + g * 512: 6144 + (g + 1) * 512],
                                                     start=False, stop=True), deps=[mm1])
            uv = p.op("ve", lambda e, g=g: e.tensor_tensor(ctx.dn_f32[:, g * 512:(g + 1) * 512],
                                                           ctx.ps[2 + g % 2][:, 0:512],
                                                           ctx.dn_f32[:, g * 512:(g + 1) * 512], ALU.mult),
                      deps=[mm2, dn_evs[g]])
            ctx.bank_last[2 + g % 2] = uv
            up_evs.append(uv)
        z2 = p.op("ve", lambda e, xa=xa, t=t: e.scalar_tensor_tensor(
            ctx.dn_f32[:], ctx.dn_f32[:], col(ctx.rs_cols, 8 + t), xa, ALU.mult, ALU.mult),
            deps=up_evs)
        z3 = p.op("gp", lambda e: e.tensor_tensor(ctx.dn_f32[:], ctx.dn_f32[:], ctx.g_all[:], ALU.mult),
                  deps=[z2, nwb])
        ra = p.op("ve", lambda e, xa=xa: e.tensor_tensor(xa, xa, ctx.dn_f32[:], ALU.add), deps=[z3])
        prev_tile = ra
        lg_free = ra
        ln = emit_layernorm(ctx, [ra] * NT, tiles=[t])
        if tile_cb is not None:
            ln_out.append(tile_cb(t, ln[0]))
        else:
            ln_out.append(ln[0])
    return ln_out



U_M128, U_M1024, U_JM, U_NJ = range(4)


def emit_attn(ctx, layer, x_deps, x_reload_dram, spill_first, cc_suffix="", pre_xt=None, pre_spills=None):
    p = ctx.p
    a = ctx.aps
    pre = "L0" if layer == 0 else "L1"
    cc_in = "cc_in" + cc_suffix
    cc_out = "cc_out" + cc_suffix
    if pre_xt is not None:
        xt = pre_xt
        spills = pre_spills or []
    else:
        xt = emit_build_xT(ctx, x_deps)
        spills = []
        if spill_first:
            for t in range(NT):
                spills.append(dma(ctx, "gp", x_reload_dram[t * 128:(t + 1) * 128, :],
                                  ctx.xsb[:, t * 2048:(t + 1) * 2048], deps=xt[16 * t + 14: 16 * t + 16]))
    sp_tok = spills[-1] if spills else None

    dc_toks = []
    if layer == 0:
        sld = dma(ctx, "sy", ctx.scol16[:], a["L0slopes"].to_broadcast([128, 16]),
                  deps=ctx.c_ready[-1:])
        m0 = p.op("ve", lambda e: e.memset(col(ctx.ucols, U_M128), -128.0))
        m1 = p.op("ve", lambda e: e.memset(col(ctx.ucols, U_M1024), -1024.0), deps=[m0])
        jm = p.op("ve", lambda e: e.tensor_scalar(col(ctx.ucols, U_JM), ctx.jcol[:], -128.0, None, ALU.add),
                  deps=ctx.c_ready[-1:] + [m1])
        nj = p.op("ve", lambda e: e.tensor_scalar(col(ctx.ucols, U_NJ), ctx.jcol[:], -1.0, None, ALU.mult),
                  deps=[jm])
        ns = p.op("ve", lambda e: e.tensor_scalar(ctx.dcols[:, DC_NEGS:DC_NEGS + 16], ctx.scol16[:], -1.0, None, ALU.mult),
                  deps=[sld, nj])
        ka = p.op("ve", lambda e: e.tensor_tensor(ctx.dcols[:, DC_KDEC:DC_KDEC + 16], ctx.scol16[:],
                                                  col(ctx.ucols, U_JM).to_broadcast([128, 16]), ALU.mult),
                  deps=[sld, ns])
        ke = p.op("sc", lambda e: e.activation(ctx.dcols[:, DC_KDEC:DC_KDEC + 16],
                                               ctx.dcols[:, DC_KDEC:DC_KDEC + 16], AF.Exp), deps=[ka])
        qa = p.op("ve", lambda e: e.tensor_tensor(ctx.dcols[:, DC_QDEC:DC_QDEC + 16], ctx.scol16[:],
                                                  col(ctx.ucols, U_NJ).to_broadcast([128, 16]), ALU.mult),
                  deps=[ka])
        qe = p.op("sc", lambda e: e.activation(ctx.dcols[:, DC_QDEC:DC_QDEC + 16],
                                               ctx.dcols[:, DC_QDEC:DC_QDEC + 16], AF.Exp), deps=[qa, ke])
        be = p.op("sc", lambda e: e.activation(ctx.dcols[:, DC_BD:DC_BD + 16], ctx.scol16[:],
                                               AF.Exp, scale=col(ctx.ucols, U_M128)), deps=[qe, m1, sld])
        dc_toks = [ke, qe, be, ns]

    bias_toks = []
    if layer == 0:
        bias_toks.append(dma(ctx, "sy", ctx.bias_bc[:], a["L0bqkv"].to_broadcast([128, 6144]),
                             deps=ctx.c_ready[-1:]))

    regs = {0: ctx.qreg, 1: ctx.kreg, 2: ctx.vreg}

    def evac_qkv(g, sub, psum, deps):
        reg = regs[g // 4]
        dst = reg[:, sub * 2048 + (g % 4) * 512: sub * 2048 + (g % 4) * 512 + 512]
        if layer == 0:
            return p.op("ve", lambda e, dst=dst, psum=psum, g=g: e.tensor_tensor(
                dst, psum, ctx.bias_bc[:, g * 512:(g + 1) * 512], ALU.add),
                deps=deps + bias_toks + ([sp_tok] if sp_tok else []) + xt[-2:])
        return p.op("sc", lambda e, dst=dst, psum=psum: e.activation(dst, psum, AF.Identity),
                    deps=deps + ([sp_tok] if sp_tok else []) + xt[-2:])

    def rope_reg(reg, prev):
        toks = []
        for t in range(NT):
            cs = t % 2
            cld = dma(ctx, "sy", ctx.cos_bf[:, cs * 128:(cs + 1) * 128],
                      a["cosbf"][t * 128:(t + 1) * 128, :], deps=prev[-1:])
            sldd = dma(ctx, "sy", ctx.sin_bf[:, cs * 128:(cs + 1) * 128],
                       a["sinbf"][t * 128:(t + 1) * 128, :], deps=[cld])
            sl = reg[:, t * 2048:(t + 1) * 2048]
            sl3 = sl.rearrange("p (h j) -> p h j", h=16)
            rot3 = ctx.rope_rot[:].rearrange("p (h j) -> p h j", h=16)
            cosb = bass.AP(ctx.cos_bf, cs * 128, [[256, 128], [0, 16], [1, 128]])
            sinb = bass.AP(ctx.sin_bf, cs * 128, [[256, 128], [0, 16], [1, 128]])
            swap = bass.AP(reg, t * 2048 + 1, [[16384, 128], [2, 1024], [-1, 2]])
            r1 = p.op("ve", lambda e, swap=swap: e.tensor_copy(ctx.rope_rot[:], swap), deps=prev + [sldd])
            r2 = p.op("ve", lambda e, sl3=sl3, cosb=cosb: e.tensor_tensor(sl3, sl3, cosb, ALU.mult), deps=[r1, sldd])
            r3 = p.op("ve", lambda e, rot3=rot3, sinb=sinb: e.tensor_tensor(rot3, rot3, sinb, ALU.mult), deps=[r2])
            r4 = p.op("ve", lambda e, sl=sl: e.tensor_tensor(sl, sl, ctx.rope_rot[:], ALU.add), deps=[r3])
            r5 = p.op("ve", lambda e, sl=sl: e.tensor_scalar_min(ctx.rope_mn[:], sl, 0.0), deps=[r4])
            r6 = p.op("sc", lambda e: e.activation(ctx.rope_rot[:], ctx.rope_mn[:], AF.Exp), deps=[r5])
            r7 = p.op("ve", lambda e, sl=sl: e.scalar_tensor_tensor(sl, sl, 0.0, ctx.rope_rot[:], ALU.max, ALU.add), deps=[r6])
            prev = [r7]
            toks.append(r7)
        return toks

    lhsT_qkv = lambda ci, sub: ctx.xT[:, ci * 1024 + sub * 128: ci * 1024 + sub * 128 + 128]
    xt_by_tile = [xt[16 * t + 14: 16 * t + 16] for t in range(NT)]
    kv = emit_proj(ctx, a[pre + "Wqkv"], 12, 8, 2, lhsT_qkv, evac_qkv,
                   g_list=range(4, 12), lhsT_deps=(),
                   lhsT_deps_fn=lambda sub: xt_by_tile[sub],
                   cast_eng="sc" if layer == 1 else "ve")

    rope_done = kv[-1]
    if layer == 1:
        rope_done = rope_reg(ctx.kreg, [kv[-1]])[-1]

    gz = p.op("ve", lambda e: e.memset(ctx.g_all[:], 0.0), deps=[rope_done] + dc_toks)
    zz = p.op("ve", lambda e: e.memset(ctx.z_all[:], 0.0), deps=[gz])
    cc_dmas = []
    last_s = {}
    SLOT = lambda pi, k: ctx.scan_bf[:, (pi * 8 + k) * 128: (pi * 8 + k) * 128 + 128]
    pv1 = [{"kd": None} for _ in range(4)]
    st_tok_h = {}
    z_tok_h = {}
    for hg in range(NH // 4):
        heads = [hg * 4 + i for i in range(4)]
        for h in heads:
            st_tok_h[h] = zz
            z_tok_h[h] = zz
        for c in range(NCH):
            ctxv = []
            for pi, h in enumerate(heads):
                gh = ctx.g_all[:, h * 128:(h + 1) * 128]
                ks = ctx.kreg[:, c * 2048 + h * 128: c * 2048 + h * 128 + 128]
                vs = ctx.vreg[:, c * 2048 + h * 128: c * 2048 + h * 128 + 128]
                abank = ctx.ps[3 + 2 * (pi % 2)][:, 0:128]
                ctxv.append((h, gh, ks, vs, abank, 3 + 2 * (pi % 2)))
            if layer == 0:
                kds = []
                for pi, (h, gh, ks, vs, abank, bidx) in enumerate(ctxv):
                    kds.append(p.op("sc", lambda e, ks=ks, h=h, pi=pi: e.activation(
                        SLOT(pi, 3), ks, AF.Identity, scale=col(ctx.dcols, DC_KDEC + h)),
                        deps=[rope_done] + dc_toks + [st_tok_h[h], pv1[pi]["kd"]]))
                for wave in range(2):
                    As = {}
                    for pi in (2 * wave, 2 * wave + 1):
                        h, gh, ks, vs, abank, bidx = ctxv[pi]
                        A = p.op("te", lambda e, vs=vs, pi=pi, abank=abank: e.matmul(
                            abank, SLOT(pi, 3), vs, start=True, stop=True),
                            deps=[kds[pi], ctx.bank_last[bidx]])
                        pv1[pi]["kd"] = A
                        As[pi] = A
                        ctx.bank_last[bidx] = A
                    for pi in (2 * wave, 2 * wave + 1):
                        h, gh, ks, vs, abank, bidx = ctxv[pi]
                        st_tok_h[h] = p.op("ve", lambda e, gh=gh, h=h, abank=abank: e.scalar_tensor_tensor(
                            gh, gh, col(ctx.dcols, DC_BD + h), abank, ALU.mult, ALU.add),
                            deps=[As[pi], st_tok_h[h]])
                        ctx.bank_last[bidx] = st_tok_h[h]
            else:
                for wave in range(2):
                    As = {}
                    for pi in (2 * wave, 2 * wave + 1):
                        h, gh, ks, vs, abank, bidx = ctxv[pi]
                        As[pi] = p.op("te", lambda e, ks=ks, vs=vs, abank=abank: e.matmul(
                            abank, ks, vs, start=True, stop=True),
                            deps=[rope_done, st_tok_h[h], ctx.bank_last[bidx]])
                        ctx.bank_last[bidx] = As[pi]
                    for pi in (2 * wave, 2 * wave + 1):
                        h, gh, ks, vs, abank, bidx = ctxv[pi]
                        st_tok_h[h] = p.op("ve", lambda e, gh=gh, abank=abank: e.tensor_tensor(
                            gh, gh, abank, ALU.add), deps=[As[pi], st_tok_h[h]])
                        ctx.bank_last[bidx] = st_tok_h[h]
                for wave in range(2):
                    Zps = {}
                    for pi in (2 * wave, 2 * wave + 1):
                        h, gh, ks, vs, abank, bidx = ctxv[pi]
                        zb = ctx.ps[pi % 2][:, 0:1]
                        Zps[pi] = p.op("te", lambda e, ks=ks, zb=zb: e.matmul(zb, ks, ctx.onecol_bf[:],
                                                                              start=True, stop=True),
                                       deps=[rope_done, ctx.bank_last[pi % 2], z_tok_h[h]])
                        ctx.bank_last[pi % 2] = Zps[pi]
                    for pi in (2 * wave, 2 * wave + 1):
                        h, gh, ks, vs, abank, bidx = ctxv[pi]
                        zb = ctx.ps[pi % 2][:, 0:1]
                        z_tok_h[h] = p.op("ve", lambda e, h=h, zb=zb: e.tensor_tensor(
                            col(ctx.z_all, h), col(ctx.z_all, h), zb, ALU.add),
                            deps=[Zps[pi], z_tok_h[h]])
                        ctx.bank_last[pi % 2] = z_tok_h[h]
        for h in heads:
            last_s[h] = st_tok_h[h]
            gh = ctx.g_all[:, h * 128:(h + 1) * 128]
            cc_dmas.append(dma(ctx, "gp", a[cc_in][h * 128:(h + 1) * 128, :], gh, deps=[st_tok_h[h]]))
            if layer == 1:
                ztr = p.op("te", lambda e, h=h: e.transpose(ctx.ps[4][0:1, 0:128], col(ctx.z_all, h),
                                                            ctx.ident_f32[:]),
                           deps=[z_tok_h[h], ctx.bank_last[4]])
                zev = p.op("sc", lambda e: e.activation(ctx.zrow[:], ctx.ps[4][0:1, 0:128], AF.Identity),
                           deps=[ztr])
                ctx.bank_last[4] = zev
                cc_dmas.append(dma(ctx, "gp", a[cc_in][2048 + h:2048 + h + 1, :], ctx.zrow[:], deps=[zev]))
            else:
                cc_dmas.append(dma(ctx, "gp", a[cc_in][2048 + h:2048 + h + 1, :], ctx.zrow[:],
                                   deps=[st_tok_h[h]] + ctx.c_ready[-1:]))


    cc = p.op("gp", lambda e: e.collective_compute(
        "AllGather", ALU.bypass, replica_groups=[list(range(NCORES))],
        ins=[a[cc_in].opt()], outs=[a[cc_out].opt()]), deps=cc_dmas, solo=True)

    qp = emit_proj(ctx, a[pre + "Wqkv"], 12, 8, 2, lhsT_qkv, evac_qkv,
                   g_list=range(0, 4), lhsT_deps=(),
                   lhsT_deps_fn=lambda sub: xt_by_tile[sub])
    q_done = qp[-1]
    if layer == 1:
        q_done = rope_reg(ctx.qreg, [qp[-1]])[-1]

    fm_toks = []
    if layer == 0:
        fm = []
        for h in range(NH):
            fm.append(p.op("ve", lambda e, h=h: e.tensor_tensor(
                ctx.fmat[:, h * 8:(h + 1) * 8], col(ctx.scol16, h).to_broadcast([128, 8]),
                ctx.dist8[:], ALU.mult), deps=[cc_dmas[-1]] + dc_toks + (fm[-1:] if fm else [])))
        fe = p.op("sc", lambda e: e.activation(ctx.fmat[:], ctx.fmat[:], AF.Exp,
                                               scale=col(ctx.ucols, U_M1024)), deps=[fm[-1]])
        pmb = bass.AP(ctx.pm8, 0, [[8, 128], [0, 16], [1, 8]])
        fm3 = ctx.fmat[:].rearrange("p (h j) -> p h j", h=16)
        fp = p.op("ve", lambda e, fm3=fm3, pmb=pmb: e.tensor_tensor(fm3, fm3, pmb, ALU.mult), deps=[fe])
        fm_toks = [fp]
    gzero = p.op("ve", lambda e: e.memset(ctx.g_all[:], 0.0), deps=[cc_dmas[-1]] + [last_s[NH - 1]])
    zzero = p.op("ve", lambda e: e.memset(ctx.z_all[:], 0.0), deps=[gzero]) if layer == 1 else gzero
    comb_tok = zzero
    wst_free0 = getattr(ctx, "_wst_free2", [None, None])
    wfl = [w for w in wst_free0 if w is not None]
    for cp in range(NCORES):
        gl = []
        for qh in range(4):
            gl.append(dma(ctx, "sy", ctx.wst[:, qh * 512:(qh + 1) * 512].rearrange("p (h j) -> p h j", h=4),
                          a[cc_out][cp * 2064 + qh * 512: cp * 2064 + (qh + 1) * 512, :]
                          .rearrange("(h p) j -> p h j", p=128),
                          deps=[cc, comb_tok] + wfl))
        wfl = []
        for h in range(NH):
            if layer == 0:
                fcol = ctx.fmat[:, h * 8 + cp: h * 8 + cp + 1]
            else:
                fcol = col(ctx.pm8, cp)
            comb_tok = p.op("ve", lambda e, h=h, fcol=fcol: e.scalar_tensor_tensor(
                ctx.g_all[:, h * 128:(h + 1) * 128], ctx.wst[:, h * 128:(h + 1) * 128],
                fcol, ctx.g_all[:, h * 128:(h + 1) * 128], ALU.mult, ALU.add),
                deps=[gl[h // 4]] + fm_toks + [comb_tok])
        wst_free = comb_tok
    ctx._wst_free2 = [comb_tok, comb_tok]
    if layer == 1:
        for cp in range(NCORES):
            zld = dma(ctx, "sy", ctx.scrT_f32[0:16, 0:128],
                      a[cc_out][cp * 2064 + 2048: cp * 2064 + 2064, :], deps=[cc, comb_tok, q_done])
            ztr = p.op("te", lambda e: e.transpose(ctx.ps[4][0:16, 0:16].transpose([0, 1]) if False else ctx.ps[4][:, 0:16],
                                                   ctx.scrT_f32[0:16, 0:128], ctx.ident_f32[0:16, 0:16]),
                       deps=[zld, ctx.bank_last[4]])
            comb_tok = p.op("ve", lambda e, cp=cp: e.scalar_tensor_tensor(
                ctx.z_all[:], ctx.ps[4][:, 0:16], col(ctx.pm8, cp), ctx.z_all[:], ALU.mult, ALU.add),
                deps=[ztr, comb_tok, zzero])
            ctx.bank_last[4] = comb_tok

    comb_tok = p.op("ve", lambda e: e.engine_nop() if False else e.nop(), deps=[comb_tok, q_done])

    comb_tok = p.op("ve", lambda e: e.nop(), deps=[comb_tok, q_done])

    oT_done = []
    SLOT = lambda pi, k: ctx.scan_bf[:, (pi * 8 + k) * 128: (pi * 8 + k) * 128 + 128]
    S_QT, S_KT, S_QDT, S_KD, S_QD, S_SD, S_SB, S_X = range(8)
    pv = [{k: None for k in ("qT", "kT", "qdT", "sd", "kd", "qd", "sbf", "zc", "dbf", "rep", "intra")}
          for _ in range(4)]
    den_sl = lambda pi: ctx.den_sb[0:1, pi * 256: pi * 256 + 128]
    deni_sl = lambda pi: ctx.den_sb[0:1, pi * 256 + 128: pi * 256 + 256]
    z_tok2 = {h: comb_tok for h in range(NH)}
    s_tok2 = {h: comb_tok for h in range(NH)}
    last_oe = None

    def tr_pair(pis, srcs, dst_k, deps_map, war_key):
        ts_ = {}
        for pi in pis:
            tb = ctx.ps_bf2[pi % 2]
            tbi = 6 + pi % 2
            ts_[pi] = p.op("te", lambda e, tb=tb, src=srcs[pi]: e.transpose(tb[:, 0:128], src,
                                                                            ctx.ident_bf[:]),
                           deps=list(deps_map[pi]) + [ctx.bank_last[tbi]])
            ctx.bank_last[tbi] = ts_[pi]
        evs = {}
        for j, pi in enumerate(pis):
            tb = ctx.ps_bf2[pi % 2]
            tbi = 6 + pi % 2
            if j % 2 == 0:
                evs[pi] = p.op("sc", lambda e, pi=pi, tb=tb, dst_k=dst_k: e.activation(
                    SLOT(pi, dst_k), tb[:, 0:128], AF.Identity), deps=[ts_[pi], pv[pi][war_key]])
            else:
                evs[pi] = p.op("ve", lambda e, pi=pi, tb=tb, dst_k=dst_k: e.tensor_copy(
                    SLOT(pi, dst_k), tb[:, 0:128]), deps=[ts_[pi], pv[pi][war_key]])
            ctx.bank_last[tbi] = evs[pi]
        return evs

    for hg in range(NH // 4):
        heads = [hg * 4 + i for i in range(4)]
        ih = {pi: [] for pi in range(4)}
        if layer == 0:
            i1s = {}
            for pi, h in enumerate(heads):
                i1s[pi] = p.op("sc", lambda e, h=h, pi=pi: e.activation(
                    SLOT(pi, S_X), ctx.delta_bf[:], AF.Exp, scale=col(ctx.dcols, DC_NEGS + h)),
                    deps=[comb_tok] + dc_toks + [pv[pi]["intra"]])
            for pi, h in enumerate(heads):
                i2_ = p.op("ve", lambda e, pi=pi: e.tensor_tensor(SLOT(pi, S_X), SLOT(pi, S_X),
                                                                  ctx.trilT_bf[:], ALU.mult),
                           deps=[i1s[pi]])
                ih[pi] = [i2_]
        for c in range(NCH):
            for pp in range(2):
                pis = [2 * pp, 2 * pp + 1]
                H = {}
                for pi in pis:
                    h = heads[pi]
                    H[pi] = (h,
                             ctx.g_all[:, h * 128:(h + 1) * 128],
                             ctx.qreg[:, c * 2048 + h * 128: c * 2048 + h * 128 + 128],
                             ctx.kreg[:, c * 2048 + h * 128: c * 2048 + h * 128 + 128],
                             ctx.vreg[:, c * 2048 + h * 128: c * 2048 + h * 128 + 128])
                e1s = tr_pair(pis, {pi: H[pi][2] for pi in pis}, S_QT,
                              {pi: [comb_tok] for pi in pis}, "qT")
                e2s = tr_pair(pis, {pi: H[pi][3] for pi in pis}, S_KT,
                              {pi: [comb_tok] for pi in pis}, "kT")
                if layer == 0:
                    qds = {}
                    for pi in pis:
                        h = heads[pi]
                        qds[pi] = p.op("ve", lambda e, pi=pi, h=h, qs=H[pi][2]: e.tensor_scalar(
                            SLOT(pi, S_QD), qs, col(ctx.dcols, DC_QDEC + h), None, ALU.mult),
                            deps=[comb_tok] + dc_toks + [pv[pi]["qd"]])
                    e3s = tr_pair(pis, {pi: SLOT(pi, S_QD) for pi in pis}, S_QDT,
                                  {pi: [qds[pi]] for pi in pis}, "qdT")
                    for pi in pis:
                        pv[pi]["qd"] = e3s[pi]
                sts, sds = {}, {}
                for pi in pis:
                    sb_ = 4 if pi % 2 == 0 else 2
                    sts[pi] = p.op("te", lambda e, pi=pi, sb_=sb_: e.matmul(
                        ctx.ps[sb_][:, 0:128], SLOT(pi, S_KT), SLOT(pi, S_QT), start=True, stop=True),
                        deps=[e1s[pi], e2s[pi], ctx.bank_last[sb_]])
                    ctx.bank_last[sb_] = sts[pi]
                for pi in pis:
                    sb_ = 4 if pi % 2 == 0 else 2
                    mask = SLOT(pi, S_X) if layer == 0 else ctx.trilT_bf[:]
                    sds[pi] = p.op("ve", lambda e, pi=pi, sb_=sb_, mask=mask: e.tensor_tensor(
                        SLOT(pi, S_SD), ctx.ps[sb_][:, 0:128], mask, ALU.mult),
                        deps=[sts[pi], pv[pi]["sd"]] + (ih[pi] if layer == 0 else []))
                    if layer == 0:
                        pv[pi]["intra"] = sds[pi]
                    ctx.bank_last[sb_] = sds[pi]
                rpes, d2s, zcs = {}, {}, {}
                if layer == 1:
                    for pi in pis:
                        h = heads[pi]
                        zcs[pi] = p.op("ve", lambda e, h=h, pi=pi: e.tensor_copy(
                            col(ctx.zcol_bf, pi), col(ctx.z_all, h)), deps=[z_tok2[h], pv[pi]["zc"]])
                    for pi in pis:
                        db = pi % 2
                        d1 = p.op("te", lambda e, pi=pi, db=db: e.matmul(
                            ctx.ps[db][0:1, 0:128], ctx.onecol_bf[:], SLOT(pi, S_SD),
                            start=True, stop=False), deps=[sds[pi], ctx.bank_last[db]])
                        d2s[pi] = p.op("te", lambda e, pi=pi, db=db: e.matmul(
                            ctx.ps[db][0:1, 0:128], col(ctx.zcol_bf, pi), SLOT(pi, S_QT),
                            start=False, stop=True), deps=[zcs[pi], e1s[pi], d1])
                        pv[pi]["zc"] = d2s[pi]
                        ctx.bank_last[db] = d2s[pi]
                    d5s = {}
                    for pi in pis:
                        db = pi % 2
                        d3 = p.op("ve", lambda e, pi=pi, db=db: e.tensor_scalar_max(
                            den_sl(pi), ctx.ps[db][0:1, 0:128], 1e-6), deps=[d2s[pi]])
                        d4 = p.op("ve", lambda e, pi=pi: e.reciprocal(deni_sl(pi), den_sl(pi)), deps=[d3])
                        d5s[pi] = p.op("ve", lambda e, pi=pi: e.tensor_copy(
                            ctx.den_bf[0:1, pi * 128:(pi + 1) * 128], deni_sl(pi)),
                            deps=[d4, pv[pi]["dbf"]])
                        ctx.bank_last[db] = d5s[pi]
                    rps = {}
                    for pi in pis:
                        db = pi % 2
                        rps[pi] = p.op("te", lambda e, pi=pi, db=db: e.matmul(
                            ctx.ps[db][:, 128:256], ctx.ones_bf[0:1, 0:128],
                            ctx.den_bf[0:1, pi * 128:(pi + 1) * 128], start=True, stop=True),
                            deps=[d5s[pi], ctx.bank_last[db]])
                        pv[pi]["dbf"] = rps[pi]
                        ctx.bank_last[db] = rps[pi]
                    for pi in pis:
                        db = pi % 2
                        rpes[pi] = p.op("sc", lambda e, pi=pi, db=db: e.activation(
                            SLOT(pi, S_X), ctx.ps[db][:, 128:256], AF.Identity),
                            deps=[rps[pi], pv[pi]["rep"]])
                        ctx.bank_last[db] = rpes[pi]
                scsts = {}
                for pi in pis:
                    h = heads[pi]
                    scsts[pi] = p.op("ve", lambda e, gh=H[pi][1], pi=pi: e.tensor_copy(SLOT(pi, S_SB), gh),
                                     deps=[s_tok2[h], pv[pi]["sbf"]])
                o2s = {}
                for pi in pis:
                    ob = 5 if pi % 2 == 0 else 3
                    o1 = p.op("te", lambda e, vs=H[pi][4], pi=pi, ob=ob: e.matmul(
                        ctx.ps[ob][:, 0:128], vs, SLOT(pi, S_SD), start=True, stop=False),
                        deps=[sds[pi], ctx.bank_last[ob]])
                    if layer == 0:
                        o2s[pi] = p.op("te", lambda e, pi=pi, ob=ob: e.matmul(
                            ctx.ps[ob][:, 0:128], SLOT(pi, S_SB), SLOT(pi, S_QDT), start=False, stop=True),
                            deps=[o1, scsts[pi], e3s[pi]])
                        pv[pi]["qdT"] = o2s[pi]
                    else:
                        o2s[pi] = p.op("te", lambda e, pi=pi, ob=ob: e.matmul(
                            ctx.ps[ob][:, 0:128], SLOT(pi, S_SB), SLOT(pi, S_QT), start=False, stop=True),
                            deps=[o1, scsts[pi], e1s[pi]])
                    pv[pi]["sd"] = o1
                    pv[pi]["sbf"] = o2s[pi]
                    pv[pi]["qT"] = sts[pi] if layer == 0 else o2s[pi]
                    pv[pi]["kT"] = sts[pi]
                    ctx.bank_last[ob] = o2s[pi]
                for pi in pis:
                    h = heads[pi]
                    ob = 5 if pi % 2 == 0 else 3
                    od = ctx.oT[:, h * 1024 + c * 128: h * 1024 + (c + 1) * 128]
                    if layer == 0:
                        oe = p.op("sc", lambda e, od=od, ob=ob: e.activation(
                            od, ctx.ps[ob][:, 0:128], AF.Identity), deps=[o2s[pi]])
                    else:
                        oe = p.op("ve", lambda e, od=od, ob=ob, pi=pi: e.tensor_tensor(
                            od, ctx.ps[ob][:, 0:128], SLOT(pi, S_X), ALU.mult),
                            deps=[o2s[pi], rpes[pi]])
                        pv[pi]["rep"] = oe
                        pv[pi]["intra"] = oe
                    ctx.bank_last[ob] = oe
                    oT_done.append(oe)
                    last_oe = oe
                if layer == 0:
                    kds2, Amms = {}, {}
                    for pi in pis:
                        h = heads[pi]
                        kds2[pi] = p.op("sc", lambda e, ks=H[pi][3], h=h, pi=pi: e.activation(
                            SLOT(pi, S_KD), ks, AF.Identity, scale=col(ctx.dcols, DC_KDEC + h)),
                            deps=[comb_tok] + dc_toks + [pv[pi]["kd"]])
                    for pi in pis:
                        ab = 5 if pi % 2 == 0 else 3
                        Amms[pi] = p.op("te", lambda e, vs=H[pi][4], pi=pi, ab=ab: e.matmul(
                            ctx.ps[ab][:, 128:256], SLOT(pi, S_KD), vs, start=True, stop=True),
                            deps=[kds2[pi], ctx.bank_last[ab]])
                        pv[pi]["kd"] = Amms[pi]
                        ctx.bank_last[ab] = Amms[pi]
                    for pi in pis:
                        h = heads[pi]
                        ab = 5 if pi % 2 == 0 else 3
                        s_tok2[h] = p.op("ve", lambda e, gh=H[pi][1], h=h, ab=ab: e.scalar_tensor_tensor(
                            gh, gh, col(ctx.dcols, DC_BD + h), ctx.ps[ab][:, 128:256], ALU.mult, ALU.add),
                            deps=[Amms[pi], scsts[pi], o2s[pi], s_tok2[h]])
                        ctx.bank_last[ab] = s_tok2[h]
                else:
                    Amms, Zmms = {}, {}
                    for pi in pis:
                        ab = 5 if pi % 2 == 0 else 3
                        Amms[pi] = p.op("te", lambda e, ks=H[pi][3], vs=H[pi][4], ab=ab: e.matmul(
                            ctx.ps[ab][:, 128:256], ks, vs, start=True, stop=True),
                            deps=[ctx.bank_last[ab], sds[pi]])
                        ctx.bank_last[ab] = Amms[pi]
                    for pi in pis:
                        h = heads[pi]
                        ab = 5 if pi % 2 == 0 else 3
                        s_tok2[h] = p.op("ve", lambda e, gh=H[pi][1], ab=ab: e.tensor_tensor(
                            gh, gh, ctx.ps[ab][:, 128:256], ALU.add),
                            deps=[Amms[pi], scsts[pi], o2s[pi], s_tok2[h]])
                        ctx.bank_last[ab] = s_tok2[h]
                    for pi in pis:
                        ab = 5 if pi % 2 == 0 else 3
                        Zmms[pi] = p.op("te", lambda e, ks=H[pi][3], ab=ab: e.matmul(
                            ctx.ps[ab][:, 384:385], ks, ctx.onecol_bf[:], start=True, stop=True),
                            deps=[ctx.bank_last[ab], d2s[pi]])
                        ctx.bank_last[ab] = Zmms[pi]
                    for pi in pis:
                        h = heads[pi]
                        ab = 5 if pi % 2 == 0 else 3
                        z_tok2[h] = p.op("ve", lambda e, h=h, ab=ab: e.tensor_tensor(
                            col(ctx.z_all, h), col(ctx.z_all, h), ctx.ps[ab][:, 384:385], ALU.add),
                            deps=[Zmms[pi], zcs[pi], z_tok2[h]])
                        ctx.bank_last[ab] = z_tok2[h]
    scan_done = last_oe
    s_tok = s_tok2[NH - 1]
    z_tok = z_tok2[NH - 1] if layer == 1 else s_tok

    xrl = []
    for t in range(NT):
        xrl.append(dma(ctx, "gp", ctx.xsb[:, t * 2048:(t + 1) * 2048],
                       x_reload_dram[t * 128:(t + 1) * 128, :],
                       deps=[scan_done, s_tok, z_tok]))
    badd = []
    if layer == 0:
        bld = dma(ctx, "sy", ctx.g_all[:], a["L0bout"].to_broadcast([128, 2048]),
                  deps=[scan_done, s_tok, z_tok])
        for t in range(NT):
            badd.append(p.op("ve", lambda e, t=t: e.tensor_tensor(
                ctx.xsb[:, t * 2048:(t + 1) * 2048], ctx.xsb[:, t * 2048:(t + 1) * 2048],
                ctx.g_all[:], ALU.add), deps=[xrl[t], bld] + (badd[-1:] if badd else [])))
    xready = badd if badd else xrl

    def evac_out(g, sub, psum, deps):
        dst = ctx.xsb[:, sub * 2048 + g * 512: sub * 2048 + (g + 1) * 512]
        return p.op("ve", lambda e, dst=dst, psum=psum: e.tensor_tensor(dst, psum, dst, ALU.add),
                    deps=deps + [xready[sub] if isinstance(xready, list) else xready])

    op_toks = emit_proj(ctx, a[pre + "Wout"], 4, 8, 2,
                        lambda ci, sub: ctx.oT[:, ci * 1024 + sub * 128: ci * 1024 + sub * 128 + 128],
                        evac_out, lhsT_deps=[scan_done])
    per_tile = {}
    for i, tok in enumerate(op_toks):
        g = i // NT
        sub = i % NT
        per_tile[sub] = tok
    ln = emit_layernorm(ctx, [per_tile[t] for t in range(NT)])
    return ln



def pack_proj_w(w, n_groups, nblk, ci_per_blk):
    K, n_out = w.shape
    assert K == nblk * ci_per_blk * 128 and n_out == n_groups * 512
    r = w.reshape(nblk, ci_per_blk, 128, n_groups, 512)
    return np.ascontiguousarray(r.transpose(3, 0, 2, 1, 4).reshape(n_groups, nblk, 128, ci_per_blk * 512))


def pack_colblocks(w):
    K, n_out = w.shape
    r = w.reshape(16, 128, n_out // 128, 128)
    return np.ascontiguousarray(r.transpose(2, 1, 0, 3).reshape(n_out // 128, 128, 16 * 128))


def host_consts():
    j = np.arange(128, dtype=np.float32)
    ident = np.eye(128, dtype=np.float32)
    delta = (j[None, :] - j[:, None]).astype(np.float32)
    trilT = (delta >= 0).astype(np.float32)
    delta = delta * trilT
    bf = ml_dtypes.bfloat16
    return {
        "c_ident": ident,
        "c_ident_bf": ident.astype(bf),
        "c_ones_bf": np.ones((128, 128), bf),
        "c_delta_bf": delta.astype(bf),
        "c_trilT_bf": trilT.astype(bf),
        "c_jcol": j.reshape(128, 1).copy(),
    }


def core_consts(core):
    pm = np.zeros((1, 8), np.float32)
    pm[0, :core] = 1.0
    dist = np.maximum(core - 1 - np.arange(8, dtype=np.float32), 0.0).reshape(1, 8)
    return {"c_pmask": pm, "c_dist": dist}


CONST_DECLS = {
    "c_ident": ([128, 128], F32), "c_ident_bf": ([128, 128], BF16),
    "c_ones_bf": ([128, 128], BF16), "c_delta_bf": ([128, 128], BF16),
    "c_trilT_bf": ([128, 128], BF16), "c_jcol": ([128, 1], F32),
    "c_pmask": ([1, 8], F32), "c_dist": ([1, 8], F32),
}


def declare(nc, name, shape, dtype, kind):
    if kind == "Internal":
        return nc.dram_tensor(name, list(shape), dtype).ap()
    return nc.dram_tensor(name, list(shape), dtype, kind=kind).ap()


def new_ctx(extra_decls):
    ctx = Ctx()
    ctx.nc = bass.Bass("TRN2", target_bir_lowering=False)
    ctx.p = Prog()
    ctx.aps = {}
    for n, (shape, dt_) in CONST_DECLS.items():
        ctx.aps[n] = declare(ctx.nc, n, shape, dt_, "ExternalInput")
    for n, (shape, dt_, kind) in extra_decls.items():
        ctx.aps[n] = declare(ctx.nc, n, shape, dt_, kind)
    alloc_sbuf(ctx)
    return ctx



def build_l1_ffn_block():
    decls = {
        "x_in": ([T, D], F32, "ExternalInput"),
        "L1g": ([128, 128, 2048], F32, "ExternalInput"),
        "L1d": ([4, 32, 128, 1024], F32, "ExternalInput"),
        "x_out": ([T, D], F32, "ExternalOutput"),
        "x_spill": ([T, D], F32, "Internal"),
    }
    ctx = new_ctx(decls)
    emit_consts(ctx)
    xl = emit_load_x(ctx, ctx.aps["x_in"])

    def out_cb(t, tok):
        return dma(ctx, "gp", ctx.aps["x_out"][t * 128:(t + 1) * 128, :],
                   ctx.scr_x[:, (t % 4) * 2048:((t % 4) + 1) * 2048], deps=[tok])

    st = emit_l1_ffn(ctx, xl, out_cb)
    ctx.p.op("gp", lambda e: None, deps=st)
    ctx.p.emit(ctx.nc)
    return ctx


def run_l1_ffn_standalone(x_full, w_gate_up, w_down):
    ctx = build_l1_ffn_block()
    hc = host_consts()
    l1g = pack_colblocks(np.ascontiguousarray(w_gate_up.T))
    l1d = pack_proj_w(np.ascontiguousarray(w_down.T), 4, 32, 2)
    x2 = x_full.reshape(SEQ, D)
    in_maps = []
    for c in range(NCORES):
        m = dict(hc)
        m.update(core_consts(c))
        m["x_in"] = np.ascontiguousarray(x2[c * T:(c + 1) * T])
        m["L1g"] = l1g
        m["L1d"] = l1d
        in_maps.append(m)
    res = run_bass_kernel_spmd(ctx.nc, in_maps, core_ids=list(range(NCORES)))
    return np.concatenate([res.results[c]["x_out"] for c in range(NCORES)], 0).reshape(1, SEQ, D)


SEQ = 8192


def make_rope_tables():
    inv = 1.0 / (10000.0 ** (np.arange(0, 128, 2, dtype=np.float64) / 128.0))
    pos = np.arange(8192, dtype=np.float64)
    fr = np.outer(pos, inv)
    emb = np.concatenate([fr, fr], -1)
    cos = np.cos(emb)
    sin = np.sin(emb)
    alt = np.ones(128)
    alt[0::2] = -1.0
    sinalt = sin * alt[None, :]
    bf = ml_dtypes.bfloat16
    return cos.astype(bf), sinalt.astype(bf)


def build_attn_block(layer):
    decls = {
        "x_in": ([T, D], F32, "ExternalInput"),
        "x_out": ([T, D], F32, "ExternalOutput"),
        "cc_in": ([2064, 128], F32, "Internal"),
        "cc_out": ([8 * 2064, 128], F32, "Internal"),
    }
    if layer == 0:
        decls.update({
            "L0Wqkv": ([12, 8, 128, 1024], F32, "ExternalInput"),
            "L0Wout": ([4, 8, 128, 1024], F32, "ExternalInput"),
            "L0bqkv": ([1, 6144], F32, "ExternalInput"),
            "L0bout": ([1, 2048], F32, "ExternalInput"),
            "L0slopes": ([1, 16], F32, "ExternalInput"),
        })
    else:
        decls.update({
            "L1Wqkv": ([12, 8, 128, 1024], F32, "ExternalInput"),
            "L1Wout": ([4, 8, 128, 1024], F32, "ExternalInput"),
            "cosbf": ([1024, 128], BF16, "ExternalInput"),
            "sinbf": ([1024, 128], BF16, "ExternalInput"),
            "x_spill": ([T, D], F32, "Internal"),
        })
    ctx = new_ctx(decls)
    emit_consts(ctx)
    xl = emit_load_x(ctx, ctx.aps["x_in"])
    reload_dram = ctx.aps["x_in"] if layer == 0 else ctx.aps["x_spill"]
    ln = emit_attn(ctx, layer, xl, reload_dram, spill_first=(layer == 1))
    st = emit_store_x(ctx, ctx.aps["x_out"], ln)
    ctx.p.op("gp", lambda e: None, deps=st)
    ctx.p.emit(ctx.nc)
    return ctx


def run_attn_standalone(layer, x_full, inputs):
    ctx = build_attn_block(layer)
    hc = host_consts()
    x2 = x_full.reshape(SEQ, D)
    shared = {}
    if layer == 0:
        shared["L0Wqkv"] = pack_proj_w(inputs["L0_attn_Wqkv"], 12, 4, 4)
        shared["L0Wout"] = pack_proj_w(inputs["L0_attn_Wout"], 4, 4, 4)
        shared["L0bqkv"] = inputs["L0_attn_bqkv"].reshape(1, 6144)
        shared["L0bout"] = inputs["L0_attn_bout"].reshape(1, 2048)
        shared["L0slopes"] = inputs["L0_attn_slopes"].reshape(1, 16)
    else:
        shared["L1Wqkv"] = pack_proj_w(np.ascontiguousarray(inputs["L1_attn_W_qkv"].T), 12, 8, 2)
        shared["L1Wout"] = pack_proj_w(np.ascontiguousarray(inputs["L1_attn_W_out"].T), 4, 8, 2)
        cosb, sinb = make_rope_tables()
        shared["_cos_full"] = cosb
        shared["_sin_full"] = sinb
    in_maps = []
    for c in range(NCORES):
        m = dict(hc)
        m.update(core_consts(c))
        m["x_in"] = np.ascontiguousarray(x2[c * T:(c + 1) * T])
        for k, v in shared.items():
            if k == "_cos_full":
                m["cosbf"] = np.ascontiguousarray(v[c * T:(c + 1) * T])
            elif k == "_sin_full":
                m["sinbf"] = np.ascontiguousarray(v[c * T:(c + 1) * T])
            else:
                m[k] = v
        in_maps.append(m)
    res = run_bass_kernel_spmd(ctx.nc, in_maps, core_ids=list(range(NCORES)))
    return np.concatenate([res.results[c]["x_out"] for c in range(NCORES)], 0).reshape(1, SEQ, D)


def pack_genes(genes):
    img = genes.reshape(32, 128, 256).transpose(1, 0, 2).reshape(128, 8192)
    return np.ascontiguousarray(img.reshape(128, 4, 2048).transpose(1, 0, 2))


def pack_wdu(Wd, Wu):
    def img(w):
        wT = np.ascontiguousarray(w.T)
        return wT.reshape(2, 128, 2048).transpose(1, 0, 2).reshape(128, 4096)
    full = np.concatenate([img(Wd), img(Wu)], axis=1)
    return np.ascontiguousarray(full.reshape(128, 4, 2048).transpose(1, 0, 2))


def build_l0_ffn_block():
    decls = {
        "x_in": ([T, D], F32, "ExternalInput"),
        "x_out": ([T, D], F32, "ExternalOutput"),
        "L0Wc": ([8, 8, 128, 1024], F32, "ExternalInput"),
        "L0genesP": ([4, 128, 2048], F32, "ExternalInput"),
        "L0wduP": ([4, 128, 2048], F32, "ExternalInput"),
        "L0nw": ([1, 2048], F32, "ExternalInput"),
        "L0temp": ([1, 1], F32, "ExternalInput"),
        "L0scale": ([1, 1], F32, "ExternalInput"),
        "lg_spill": ([T, 4096], F32, "Internal"),
    }
    ctx = new_ctx(decls)
    emit_consts(ctx)
    xl = emit_load_x(ctx, ctx.aps["x_in"])
    ln = emit_l0_ffn(ctx, xl)
    st = emit_store_x(ctx, ctx.aps["x_out"], ln)
    ctx.p.op("gp", lambda e: None, deps=st)
    ctx.p.emit(ctx.nc)
    return ctx


def l0_ffn_shared(inputs):
    return {
        "L0Wc": pack_proj_w(np.ascontiguousarray(inputs["L0_ffn_Wc"].T), 8, 8, 2),
        "L0genesP": pack_genes(np.asarray(inputs["L0_ffn_genes"])),
        "L0wduP": pack_wdu(np.asarray(inputs["L0_ffn_Wd"]), np.asarray(inputs["L0_ffn_Wu"])),
        "L0nw": inputs["L0_ffn_norm_w"].reshape(1, 2048),
        "L0temp": np.asarray(inputs["L0_ffn_temperature"], np.float32).reshape(1, 1),
        "L0scale": np.asarray(inputs["L0_ffn_scale"], np.float32).reshape(1, 1),
    }


def run_l0_ffn_standalone(x_full, inputs):
    ctx = build_l0_ffn_block()
    hc = host_consts()
    shared = l0_ffn_shared(inputs)
    x2 = x_full.reshape(SEQ, D)
    in_maps = []
    for c in range(NCORES):
        m = dict(hc)
        m.update(core_consts(c))
        m["x_in"] = np.ascontiguousarray(x2[c * T:(c + 1) * T])
        m.update(shared)
        in_maps.append(m)
    res = run_bass_kernel_spmd(ctx.nc, in_maps, core_ids=list(range(NCORES)))
    return np.concatenate([res.results[c]["x_out"] for c in range(NCORES)], 0).reshape(1, SEQ, D)


def attn_shared(layer, inputs):
    if layer == 0:
        return {
            "L0Wqkv": pack_proj_w(np.asarray(inputs["L0_attn_Wqkv"]), 12, 8, 2),
            "L0Wout": pack_proj_w(np.asarray(inputs["L0_attn_Wout"]), 4, 8, 2),
            "L0bqkv": inputs["L0_attn_bqkv"].reshape(1, 6144),
            "L0bout": inputs["L0_attn_bout"].reshape(1, 2048),
            "L0slopes": inputs["L0_attn_slopes"].reshape(1, 16),
        }
    return {
        "L1Wqkv": pack_proj_w(np.ascontiguousarray(inputs["L1_attn_W_qkv"].T), 12, 8, 2),
        "L1Wout": pack_proj_w(np.ascontiguousarray(inputs["L1_attn_W_out"].T), 4, 8, 2),
    }


def _run_block(ctx, x_full, shared, percore_extra=None):
    hc = host_consts()
    x2 = np.ascontiguousarray(x_full.reshape(SEQ, D).astype(np.float32, copy=False))
    in_maps = []
    for c in range(NCORES):
        m = dict(hc)
        m.update(core_consts(c))
        m["x_in"] = np.ascontiguousarray(x2[c * T:(c + 1) * T])
        m.update(shared)
        if percore_extra is not None:
            m.update(percore_extra(c))
        in_maps.append(m)
    res = run_bass_kernel_spmd(ctx.nc, in_maps, core_ids=list(range(NCORES)))
    return np.concatenate([res.results[c]["x_out"] for c in range(NCORES)], 0)


_CTX_CACHE = {}


def _get(name, builder):
    if name not in _CTX_CACHE:
        _CTX_CACHE[name] = builder()
    return _CTX_CACHE[name]


def build_fused():
    decls = {
        "x_in": ([T, D], F32, "ExternalInput"),
        "x_out": ([T, D], F32, "ExternalOutput"),
        "L0Wqkv": ([12, 8, 128, 1024], F32, "ExternalInput"),
        "L0Wout": ([4, 8, 128, 1024], F32, "ExternalInput"),
        "L0bqkv": ([1, 6144], F32, "ExternalInput"),
        "L0bout": ([1, 2048], F32, "ExternalInput"),
        "L0slopes": ([1, 16], F32, "ExternalInput"),
        "L0Wc": ([8, 8, 128, 1024], F32, "ExternalInput"),
        "L0genesP": ([4, 128, 2048], F32, "ExternalInput"),
        "L0wduP": ([4, 128, 2048], F32, "ExternalInput"),
        "L0nw": ([1, 2048], F32, "ExternalInput"),
        "L0temp": ([1, 1], F32, "ExternalInput"),
        "L0scale": ([1, 1], F32, "ExternalInput"),
        "L1Wqkv": ([12, 8, 128, 1024], F32, "ExternalInput"),
        "L1Wout": ([4, 8, 128, 1024], F32, "ExternalInput"),
        "cosbf": ([1024, 128], BF16, "ExternalInput"),
        "sinbf": ([1024, 128], BF16, "ExternalInput"),
        "L1g": ([128, 128, 2048], F32, "ExternalInput"),
        "L1d": ([4, 32, 128, 1024], F32, "ExternalInput"),
        "x_spill": ([T, D], F32, "Internal"),
        "lg_spill": ([T, 4096], F32, "Internal"),
        "cc_in0": ([2064, 128], F32, "Internal"),
        "cc_out0": ([8 * 2064, 128], F32, "Internal"),
        "cc_in1": ([2064, 128], F32, "Internal"),
        "cc_out1": ([8 * 2064, 128], F32, "Internal"),
    }
    ctx = new_ctx(decls)
    emit_consts(ctx)
    xl = emit_load_x(ctx, ctx.aps["x_in"])
    ln0 = emit_attn(ctx, 0, xl, ctx.aps["x_in"], spill_first=False, cc_suffix="0")
    xt1 = []
    spl1 = []

    def cb(t, tok):
        xt1.extend(emit_build_xT(ctx, [tok] * NT, tiles=[t]))
        spl1.append(dma(ctx, "gp", ctx.aps["x_spill"][t * 128:(t + 1) * 128, :],
                        ctx.xsb[:, t * 2048:(t + 1) * 2048],
                        deps=xt1[16 * t + 14: 16 * t + 16]))
        return tok

    ln1 = emit_l0_ffn(ctx, ln0, tile_cb=cb)
    ln2 = emit_attn(ctx, 1, ln1, ctx.aps["x_spill"], spill_first=True, cc_suffix="1",
                    pre_xt=xt1, pre_spills=spl1)

    def out_cb(t, tok):
        return dma(ctx, "gp", ctx.aps["x_out"][t * 128:(t + 1) * 128, :],
                   ctx.scr_x[:, (t % 4) * 2048:((t % 4) + 1) * 2048], deps=[tok])

    st = emit_l1_ffn(ctx, ln2, out_cb)
    ctx.p.op("gp", lambda e: None, deps=st)
    ctx.p.emit(ctx.nc)
    return ctx


def kernel_fused(inputs):
    ctx = _get("fused", build_fused)
    cosb, sinb = make_rope_tables()
    shared = {}
    shared.update(attn_shared(0, inputs))
    shared.update(attn_shared(1, inputs))
    shared.update(l0_ffn_shared(inputs))
    shared["L1g"] = pack_colblocks(np.ascontiguousarray(inputs["L1_ffn_w_gate_up"].T))
    shared["L1d"] = pack_proj_w(np.ascontiguousarray(inputs["L1_ffn_w_down"].T), 4, 32, 2)
    x = _run_block(ctx, np.asarray(inputs["L0_attn_x"], np.float32), shared,
                   percore_extra=lambda c: {
                       "cosbf": np.ascontiguousarray(cosb[c * T:(c + 1) * T]),
                       "sinbf": np.ascontiguousarray(sinb[c * T:(c + 1) * T])})
    return x.reshape(1, SEQ, D)


FUSED = True


def kernel(**inputs):
    if FUSED:
        return kernel_fused(inputs)
    x = np.asarray(inputs["L0_attn_x"], np.float32)
    cosb, sinb = make_rope_tables()

    ctx = _get("a0", lambda: build_attn_block(0))
    x = _run_block(ctx, x, attn_shared(0, inputs))

    ctx = _get("f0", build_l0_ffn_block)
    x = _run_block(ctx, x, l0_ffn_shared(inputs))

    ctx = _get("a1", lambda: build_attn_block(1))
    x = _run_block(ctx, x, attn_shared(1, inputs),
                   percore_extra=lambda c: {
                       "cosbf": np.ascontiguousarray(cosb[c * T:(c + 1) * T]),
                       "sinbf": np.ascontiguousarray(sinb[c * T:(c + 1) * T])})

    ctx = _get("f1", build_l1_ffn_block)
    l1g = pack_colblocks(np.ascontiguousarray(inputs["L1_ffn_w_gate_up"].T))
    l1d = pack_proj_w(np.ascontiguousarray(inputs["L1_ffn_w_down"].T), 4, 32, 2)
    x = _run_block(ctx, x, {"L1g": l1g, "L1d": l1d})
    return x.reshape(1, SEQ, D)


# version 23
# speedup vs baseline: 1.1249x; 1.1249x over previous
import numpy as np
import ml_dtypes
import concourse.bass as bass
import concourse.mybir as mybir
from concourse.bass_utils import run_bass_kernel_spmd

F32 = mybir.dt.float32
BF16 = mybir.dt.bfloat16
AF = mybir.ActivationFunctionType
ALU = mybir.AluOpType
AX = mybir.AxisListType.X

NCORES = 8
T = 1024
NT = 8
D = 2048
NH = 16
HD = 128
CH = 128
NCH = T // CH

ENGS = ("sy", "ve", "sc", "te", "gp")

C_S, C_Q, C_MEAN, C_VAR, C_TMP, C_RSTD, C_NB, C_EPS, C_NHALF, C_EPS2, C_M, C_Z, C_NM, C_B2, C_V32, C_ITMP = range(16)
DC_KDEC = 0
DC_QDEC = 16
DC_BD = 32
DC_NEGS = 48


NDMASEM = 12


class Item:
    __slots__ = ("eng", "fn", "deps", "amt", "needed", "sem_val", "idx", "sem")

    def __init__(self, eng, fn, deps, amt, idx, sem):
        self.eng = eng
        self.fn = fn
        self.deps = deps
        self.amt = amt
        self.needed = False
        self.sem_val = None
        self.idx = idx
        self.sem = sem


class Prog:
    def __init__(self):
        self.items = []
        self.dma_cnt = {}
        self.dma_prev = {}
        self.solo_cnt = 0

    def op(self, eng, fn, deps=(), dma=False, solo=False):
        deps = [d for d in deps if d is not None]
        if dma:
            i = self.dma_cnt.get(eng, 0)
            self.dma_cnt[eng] = i + 1
            sem = f"{eng}d{i % NDMASEM}"
            prev = self.dma_prev.get(sem)
            if prev is not None:
                deps = deps + [prev]
            amt = 16
        elif solo:
            sem = f"solo{self.solo_cnt}"
            self.solo_cnt += 1
            amt = 1
        else:
            sem = eng
            amt = 1
        it = Item(eng, fn, deps, amt, len(self.items), sem)
        if dma:
            self.dma_prev[sem] = it
        self.items.append(it)
        return it

    def emit(self, nc):
        for it in self.items:
            if it.amt == 16 or it.sem.startswith("solo"):
                it.needed = True
            for d in it.deps:
                d.needed = True
        cnt = {}
        for it in self.items:
            if it.needed:
                cnt[it.sem] = cnt.get(it.sem, 0) + it.amt
                it.sem_val = cnt[it.sem]
        from contextlib import ExitStack
        es = ExitStack()
        all_sems = sorted(set([it.sem for it in self.items if it.needed] + list(ENGS)))
        sems = {sn: es.enter_context(nc.semaphore("sem_" + sn)) for sn in all_sems}
        by_eng = {e: [it for it in self.items if it.eng == e] for e in ENGS}

        def mk(eng_name):
            def emitter(eng):
                waited = {}
                for it in by_eng[eng_name]:
                    for d in it.deps:
                        if d.sem == "te" and eng_name == "te":
                            continue
                        if d.sem_val is not None and d.sem_val > waited.get(d.sem, 0):
                            eng.wait_ge(sems[d.sem], d.sem_val)
                            waited[d.sem] = d.sem_val
                    inst = it.fn(eng)
                    if it.needed and inst is None:
                        inst = eng.nop()
                    if it.needed:
                        inst.then_inc(sems[it.sem], it.amt)
                return None
            return emitter

        with es, nc.Block() as block:
            block.sync(mk("sy"))
            block.vector(mk("ve"))
            block.scalar(mk("sc"))
            block.tensor(mk("te"))
            block.gpsimd(mk("gp"))


class Ctx:
    pass


SB_BASE = 16544
OFF_XSB = SB_BASE + 0x00000
OFF_XT = SB_BASE + 0x10000
OFF_SCR = SB_BASE + 0x18000
OFF_WST = SB_BASE + 0x2A000
OFF_WBF = SB_BASE + 0x2C000
OFF_SPARE = SB_BASE + 0x2E000
OFF_CONST = SB_BASE + 0x30000


def alloc_sbuf(ctx):
    A = ctx.nc.alloc_sbuf_tensor_at
    ctx.xsb = A("xsb", [128, 16384], F32, offset=OFF_XSB)
    ctx.qreg = A("qreg", [128, 16384], BF16, offset=OFF_XSB)
    ctx.kreg = A("kreg", [128, 16384], BF16, offset=OFF_XSB + 0x8000)
    ctx.h8T = A("h8T", [128, 32768], BF16, offset=OFF_XSB)
    ctx.xT = A("xT", [128, 16384], BF16, offset=OFF_XT)
    ctx.vreg = A("vreg", [128, 16384], BF16, offset=OFF_SCR)
    ctx.oT = A("oT", [128, 16384], BF16, offset=OFF_SCR + 0x8000)
    ctx.bias_bc = A("bias_bc", [128, 6144], F32, offset=OFF_SCR + 0x8000)
    ctx.scrT_bf = A("scrT_bf", [128, 4096], BF16, offset=OFF_SCR + 0x10000)
    ctx.scrT_f32 = A("scrT_f32", [128, 2048], F32, offset=OFF_SCR + 0x10000)
    ctx.xn_f32 = A("xn_f32", [128, 2048], F32, offset=OFF_SCR)
    ctx.lstage = A("lstage", [128, 1024], F32, offset=OFF_SCR + 0x2000)
    ctx.lg_f32 = A("lg_f32", [128, 4096], F32, offset=OFF_SCR)
    ctx.work_f32 = A("work_f32", [128, 4096], F32, offset=OFF_SCR + 0x4000)
    ctx.m_bf = A("m_bf", [128, 4096], BF16, offset=OFF_SCR + 0x4000)
    ctx.p_bf = A("p_bf", [128, 4096], BF16, offset=OFF_SCR + 0x6000)
    ctx.xnb_bf = A("xnb_bf", [128, 2048], BF16, offset=OFF_SCR + 0x6000)
    ctx.genes_bf = A("genes_bf", [128, 8192], BF16, offset=OFF_SCR + 0x8000)
    ctx.wdu_bf = A("wdu_bf", [128, 8192], BF16, offset=OFF_SCR + 0xC000)
    ctx.exprT_bf = A("exprT_bf", [128, 512], BF16, offset=OFF_SCR + 0x11000)
    ctx.dn_f32 = A("dn_f32", [128, 2048], F32, offset=OFF_WST)
    ctx.scr_x = A("scr_x", [128, 8192], F32, offset=OFF_SCR)
    ctx.rope_rot = A("rope_rot", [128, 2048], BF16, offset=OFF_SCR + 0x10000)
    ctx.rope_mn = A("rope_mn", [128, 2048], BF16, offset=OFF_SCR + 0x10000 + 0x1000)
    ctx.wst = A("wst", [128, 2048], F32, offset=OFF_WST)
    ctx.wbf = A("wbf", [128, 4096], BF16, offset=OFF_WBF)
    ctx.g_all = A("g_all", [128, 2048], F32, offset=OFF_SPARE)
    ctx.spare_bf = A("spare_bf", [128, 4096], BF16, offset=OFF_SPARE)


    off = OFF_CONST

    def C(name, n, dtype, parts=128):
        nonlocal off
        t = A(name, [parts, n], dtype, offset=off)
        off += n * (2 if dtype == BF16 else 4)
        off = (off + 31) & ~31
        return t

    ctx.ident_f32 = C("ident_f32", 128, F32)
    ctx.ident_bf = C("ident_bf", 128, BF16)
    ctx.ones_bf = C("ones_bf", 128, BF16)
    ctx.delta_bf = C("delta_bf", 128, BF16)
    ctx.trilT_bf = C("trilT_bf", 128, BF16)
    ctx.iota_bf = C("iota_bf", 128, BF16)
    ctx.cols = C("cols", 16, F32)
    ctx.dcols = C("dcols", 64, F32)
    ctx.pm8 = C("pm8", 8, F32)
    ctx.dist8 = C("dist8", 8, F32)
    ctx.scol16 = C("scol16", 16, F32)
    ctx.jcol = C("jcol", 1, F32)
    ctx.fmat = C("fmat", 128, F32)
    ctx.zcol = C("zcol", 1, F32)
    ctx.zcol_bf = C("zcol_bf", 4, BF16)
    ctx.zrow = C("zrow", 128, F32, parts=1)
    ctx.onecol_bf = C("onecol_bf", 1, BF16)
    ctx.den_sb = C("den_sb", 1024, F32, parts=1)
    ctx.den_bf = C("den_bf", 512, BF16, parts=1)
    ctx.rep_bf = C("rep_bf", 128, BF16)
    ctx.vals32 = C("vals32", 32, F32)
    ctx.rs_cols = C("rs_cols", 16, F32)
    ctx.sq_trash = C("sq_trash", 2048, BF16)
    ctx.tcol = C("tcol", 4, F32)
    ctx.sccol = C("sccol", 4, F32)
    ctx.z_all = C("z_all", 16, F32)
    ctx.ucols = C("ucols", 8, F32)
    ctx.cos_bf = C("cos_bf", 256, BF16)
    ctx.sin_bf = C("sin_bf", 256, BF16)
    assert off <= 229376, off

    PS = ctx.nc.alloc_psum_tensor
    ctx.ps = [PS(f"ps{i}", [128, 512], F32) for i in range(6)]
    ctx.ps_bf2 = [PS(f"ps_bf{i}", [128, 1024], BF16) for i in range(2)]
    ctx.ps_bf = ctx.ps_bf2[0]
    ctx.scan_bf = ctx.nc.alloc_sbuf_tensor_at("scan_bf", [128, 4096], BF16, offset=OFF_SCR + 0x10000)
    ctx.bank_last = [None] * 8


def col(t, i):
    return t[:, i:i + 1]


def dma(ctx, q, out_ap, in_ap, deps=()):
    return ctx.p.op(q, lambda e, o=out_ap, i=in_ap: e.dma_start(out=o, in_=i), deps, dma=True)



def emit_consts(ctx):
    a = ctx.aps
    p = ctx.p
    tk = []
    tk.append(dma(ctx, "sy", ctx.ident_f32[:], a["c_ident"]))
    tk.append(dma(ctx, "sy", ctx.ident_bf[:], a["c_ident_bf"]))
    tk.append(dma(ctx, "sy", ctx.ones_bf[:], a["c_ones_bf"]))
    tk.append(dma(ctx, "sy", ctx.delta_bf[:], a["c_delta_bf"]))
    tk.append(dma(ctx, "sy", ctx.trilT_bf[:], a["c_trilT_bf"]))
    tk.append(dma(ctx, "sy", ctx.iota_bf[:], a["c_iota_bf"]))
    tk.append(dma(ctx, "sy", ctx.jcol[:], a["c_jcol"]))
    tk.append(dma(ctx, "sy", ctx.pm8[:], a["c_pmask"].to_broadcast([128, 8])))
    tk.append(dma(ctx, "sy", ctx.dist8[:], a["c_dist"].to_broadcast([128, 8])))
    m0 = p.op("ve", lambda e: e.memset(ctx.cols[:], 0.0))
    m1 = p.op("ve", lambda e: e.memset(col(ctx.cols, C_EPS), 1e-5), deps=[m0])
    m2 = p.op("ve", lambda e: e.memset(col(ctx.cols, C_NHALF), -0.5), deps=[m1])
    m3 = p.op("ve", lambda e: e.memset(col(ctx.cols, C_EPS2), float(np.finfo(np.float32).eps)), deps=[m2])
    m4 = p.op("ve", lambda e: e.memset(ctx.onecol_bf[:], 1.0), deps=[m3])
    m5 = p.op("ve", lambda e: e.memset(ctx.zrow[:], 0.0), deps=[m4])
    allc = tk + [m0, m1, m2, m3, m4, m5]
    bars = [p.op(e, lambda eng: None, deps=allc) for e in ENGS]
    ctx.c_ready = [bars[-1]]
    return ctx.c_ready


def emit_load_x(ctx, src_dram):
    toks = []
    for t in range(NT):
        toks.append(dma(ctx, "sy", ctx.xsb[:, t * 2048:(t + 1) * 2048],
                        src_dram[t * 128:(t + 1) * 128, :]))
    return toks


def emit_store_x(ctx, dst_dram, tile_deps, src=None, tile_off=lambda t: t * 2048):
    src = src if src is not None else ctx.xsb
    toks = []
    for t in range(NT):
        dep = tile_deps[t] if isinstance(tile_deps, list) else tile_deps
        toks.append(dma(ctx, "gp", dst_dram[t * 128:(t + 1) * 128, :],
                        src[:, tile_off(t):tile_off(t) + 2048],
                        deps=[dep] if not isinstance(dep, list) else dep))
    return toks


def emit_build_xT(ctx, tile_deps, src=None, src_off=lambda t, ci: t * 2048 + ci * 128,
                  dst=None, dst_off=lambda t, ci: ci * 1024 + t * 128, tiles=range(NT),
                  extra_deps=()):
    src = src if src is not None else ctx.xsb
    dst = dst if dst is not None else ctx.xT
    p = ctx.p
    toks = []
    for t in tiles:
        dep = tile_deps[t] if isinstance(tile_deps, list) else tile_deps
        dd = ([dep] if not isinstance(dep, list) else list(dep)) + list(extra_deps)
        for ci in range(16):
            bi = 4 + (ci % 2)
            bank = ctx.ps[bi]
            tr = p.op("te", lambda e, t=t, ci=ci, bank=bank: e.transpose(
                bank[:, 0:128], src[:, src_off(t, ci):src_off(t, ci) + 128], ctx.ident_f32[:]),
                deps=dd + ([ctx.bank_last[bi]] if ctx.bank_last[bi] else []))
            eng = "sc" if ci % 2 == 0 else "ve"
            if eng == "sc":
                ev = p.op("sc", lambda e, t=t, ci=ci, bank=bank: e.activation(
                    dst[:, dst_off(t, ci):dst_off(t, ci) + 128], bank[:, 0:128], AF.Identity), deps=[tr])
            else:
                ev = p.op("ve", lambda e, t=t, ci=ci, bank=bank: e.tensor_copy(
                    dst[:, dst_off(t, ci):dst_off(t, ci) + 128], bank[:, 0:128]), deps=[tr])
            ctx.bank_last[bi] = ev
            toks.append(ev)
    return toks


def emit_proj(ctx, w_ap, n_groups, nblk, ci_per_blk, lhsT_fn, evac_fn, subs=range(NT),
              g_list=None, lhsT_deps=(), banks=(0, 1, 2, 3), lhsT_deps_fn=None, cast_eng="ve"):
    p = ctx.p
    subs = list(subs)
    nb = len(banks)
    halves = [subs[i:i + nb] for i in range(0, len(subs), nb)]
    if g_list is None:
        g_list = range(n_groups)
    assert ci_per_blk == 2
    out_toks = []
    wst_free = getattr(ctx, "_wst_free2", [None, None])
    wbf_free = getattr(ctx, "_wbf_free4", [None, None, None, None])
    prev_evac = {}
    bcount = getattr(ctx, "_proj_bcount", 0)
    for g in g_list:
        for half in halves:
            mm_last = {}
            for b in range(nblk):
                ws = bcount % 2
                slot = bcount % 4
                bcount += 1
                ld = dma(ctx, "sy", ctx.wst[:, ws * 1024:(ws + 1) * 1024], w_ap[g, b],
                         deps=[wst_free[ws]] if wst_free[ws] else [])
                if cast_eng == "sc":
                    cast = p.op("sc", lambda e, slot=slot, ws=ws: e.activation(
                        ctx.wbf[:, slot * 1024:(slot + 1) * 1024],
                        ctx.wst[:, ws * 1024:(ws + 1) * 1024], AF.Identity),
                        deps=[ld] + ([wbf_free[slot]] if wbf_free[slot] else []))
                else:
                    cast = p.op(cast_eng, lambda e, slot=slot, ws=ws: e.tensor_copy(
                        ctx.wbf[:, slot * 1024:(slot + 1) * 1024],
                        ctx.wst[:, ws * 1024:(ws + 1) * 1024]),
                        deps=[ld] + ([wbf_free[slot]] if wbf_free[slot] else []))
                wst_free[ws] = cast
                last_mm_blk = None
                for sub in half:
                    bank = ctx.ps[banks[half.index(sub)]]
                    for cii in range(ci_per_blk):
                        ci = b * ci_per_blk + cii
                        start = (b == 0 and cii == 0)
                        stop = (b == nblk - 1 and cii == ci_per_blk - 1)
                        deps = [cast]
                        if start:
                            deps += list(lhsT_deps)
                            if lhsT_deps_fn is not None:
                                deps += [d for d in lhsT_deps_fn(sub) if d is not None]
                            pe_ = prev_evac.get(sub)
                            if pe_ is not None:
                                deps += [pe_]
                            bl = ctx.bank_last[banks[half.index(sub)]]
                            if bl is not None:
                                deps += [bl]
                        mm = p.op("te", lambda e, sub=sub, ci=ci, cii=cii, slot=slot, bank=bank,
                                  start=start, stop=stop: e.matmul(
                            bank[:, 0:512], lhsT_fn(ci, sub),
                            ctx.wbf[:, slot * 1024 + cii * 512: slot * 1024 + (cii + 1) * 512],
                            start=start, stop=stop), deps=deps)
                        mm_last[sub] = mm
                        last_mm_blk = mm
                wbf_free[slot] = last_mm_blk
            for sub in half:
                ev = evac_fn(g, sub, ctx.ps[banks[half.index(sub)]][:, 0:512], [mm_last[sub]])
                prev_evac[sub] = ev
                ctx.bank_last[banks[half.index(sub)]] = ev
                out_toks.append(ev)
    ctx._proj_bcount = bcount
    ctx._wst_free2 = wst_free
    ctx._wbf_free4 = wbf_free
    return out_toks


def emit_layernorm(ctx, tile_deps, x_tensor=None, tile_off=lambda t: t * 2048, tiles=range(NT)):
    x_tensor = x_tensor if x_tensor is not None else ctx.xsb
    p = ctx.p
    cc = ctx.cols
    out = []
    prev = None
    for t in tiles:
        dep = tile_deps[t] if isinstance(tile_deps, list) else tile_deps
        dd = [dep] if not isinstance(dep, list) else list(dep)
        if prev is not None:
            dd = dd + [prev]
        xa = x_tensor[:, tile_off(t): tile_off(t) + 2048]
        red = p.op("ve", lambda e, xa=xa: e.tensor_reduce(col(cc, C_S), xa, AX, ALU.add), deps=dd)
        sq = p.op("sc", lambda e, xa=xa: e.activation(ctx.sq_trash[:], xa, AF.Square,
                                                      accum_out=col(cc, C_Q)), deps=dd)
        m1 = p.op("ve", lambda e: e.tensor_scalar(col(cc, C_MEAN), col(cc, C_S), 1.0 / 2048, None, ALU.mult), deps=[red])
        v1 = p.op("ve", lambda e: e.tensor_scalar(col(cc, C_VAR), col(cc, C_Q), 1.0 / 2048, None, ALU.mult), deps=[sq])
        t1 = p.op("ve", lambda e: e.tensor_tensor(col(cc, C_TMP), col(cc, C_MEAN), col(cc, C_MEAN), ALU.mult), deps=[m1])
        v2 = p.op("ve", lambda e: e.tensor_sub(col(cc, C_VAR), col(cc, C_VAR), col(cc, C_TMP)), deps=[v1, t1])
        l1 = p.op("sc", lambda e: e.activation(col(cc, C_RSTD), col(cc, C_VAR), AF.Ln, bias=col(cc, C_EPS)), deps=[v2])
        e1 = p.op("sc", lambda e: e.activation(col(cc, C_RSTD), col(cc, C_RSTD), AF.Exp, scale=col(cc, C_NHALF)), deps=[l1])
        n1 = p.op("ve", lambda e: e.tensor_scalar(col(cc, C_NB), col(cc, C_MEAN), -1.0, None, ALU.mult), deps=[m1])
        n2 = p.op("ve", lambda e: e.tensor_tensor(col(cc, C_NB), col(cc, C_NB), col(cc, C_RSTD), ALU.mult), deps=[n1, e1])
        ap_ = p.op("sc", lambda e, xa=xa: e.activation(xa, xa, AF.Identity, bias=col(cc, C_NB),
                                                       scale=col(cc, C_RSTD)), deps=[n2, red, sq])
        prev = ap_
        out.append(ap_)
    return out



def emit_l1_ffn(ctx, x_deps, out_cb):
    p = ctx.p
    a = ctx.aps
    xt = emit_build_xT(ctx, x_deps)
    spill = []
    for t in range(NT):
        spill.append(dma(ctx, "gp", a["x_spill"][t * 128:(t + 1) * 128, :],
                        ctx.xsb[:, t * 2048:(t + 1) * 2048], deps=xt[16 * t + 14: 16 * t + 16]))
    out_toks = [None] * NT
    h8_free = None
    slot_free = [None] * 4
    wst_free = getattr(ctx, "_wst_free2", [None, None])
    wbf_free = getattr(ctx, "_wbf_free4", [None, None, None, None])
    bcount = getattr(ctx, "_proj_bcount", 0)
    for s in range(2):
        h8_last = None
        sg_tok = [None, None]
        sg_free = [None, None]
        for oc in range(64):
            for half in range(2):
                ocol = oc if half == 0 else 64 + oc
                bank = ctx.ps[4 + half]
                mm = None
                for b2 in range(2):
                    ws = bcount % 2
                    slot = bcount % 4
                    bcount += 1
                    ld = dma(ctx, "sy", ctx.wst[:, ws * 1024:(ws + 1) * 1024],
                             a["L1g"][ocol][:, b2 * 1024:(b2 + 1) * 1024],
                             deps=[wst_free[ws]] if wst_free[ws] else [])
                    cast = p.op("ve", lambda e, slot=slot, ws=ws: e.tensor_copy(
                        ctx.wbf[:, slot * 1024:(slot + 1) * 1024],
                        ctx.wst[:, ws * 1024:(ws + 1) * 1024]),
                        deps=[ld] + ([wbf_free[slot]] if wbf_free[slot] else []))
                    wst_free[ws] = cast
                    for cii in range(8):
                        ci = b2 * 8 + cii
                        deps = [cast]
                        if ci == 0:
                            deps += xt[-2:] + [spill[-1]]
                            if h8_free is not None:
                                deps += [h8_free]
                            bl = ctx.bank_last[4 + half]
                            if bl is not None:
                                deps += [bl]
                        mm = p.op("te", lambda e, ci=ci, s=s, slot=slot, cii=cii, bank=bank,
                                  first=(ci == 0), last=(ci == 15): e.matmul(
                            bank[:, 0:512],
                            ctx.wbf[:, slot * 1024 + cii * 128: slot * 1024 + (cii + 1) * 128],
                            ctx.xT[:, ci * 1024 + s * 512: ci * 1024 + (s + 1) * 512],
                            start=first, stop=last), deps=deps)
                    wbf_free[slot] = mm
                if half == 0:
                    sg = p.op("sc", lambda e, oc=oc: e.activation(
                        ctx.scrT_bf[:, (oc % 2) * 512:(oc % 2) * 512 + 512],
                        ctx.ps[4][:, 0:512], AF.Silu),
                        deps=[mm] + ([sg_free[oc % 2]] if sg_free[oc % 2] else []))
                    sg_tok[oc % 2] = sg
                    ctx.bank_last[4] = sg
                else:
                    hv = p.op("ve", lambda e, oc=oc: e.tensor_tensor(
                        ctx.h8T[:, oc * 512:(oc + 1) * 512],
                        ctx.ps[5][:, 0:512],
                        ctx.scrT_bf[:, (oc % 2) * 512:(oc % 2) * 512 + 512], ALU.mult),
                        deps=[mm, sg_tok[oc % 2]])
                    sg_free[oc % 2] = hv
                    ctx.bank_last[5] = hv
                    h8_last = hv
        xrl = []
        for tt in range(4):
            t = s * 4 + tt
            dd = [spill[t]]
            if slot_free[tt] is not None:
                dd.append(slot_free[tt])
            xrl.append(dma(ctx, "gp", ctx.scr_x[:, tt * 2048:(tt + 1) * 2048],
                           a["x_spill"][t * 128:(t + 1) * 128, :], deps=dd))
        def lhsT_dn(ic, sub):
            return ctx.h8T[:, ic * 512 + (sub % 4) * 128: ic * 512 + (sub % 4) * 128 + 128]

        def evac_dn(g, sub, psum, deps):
            tt = sub % 4
            return p.op("ve", lambda e, g=g, tt=tt, psum=psum: e.tensor_tensor(
                ctx.scr_x[:, tt * 2048 + g * 512: tt * 2048 + (g + 1) * 512], psum,
                ctx.scr_x[:, tt * 2048 + g * 512: tt * 2048 + (g + 1) * 512], ALU.add),
                deps=deps + [xrl[tt]])

        ctx._wst_free2 = wst_free
        ctx._wbf_free4 = wbf_free
        ctx._proj_bcount = bcount
        dn = emit_proj(ctx, a["L1d"], 4, 32, 2, lhsT_dn, evac_dn,
                       subs=[s * 4 + i for i in range(4)], lhsT_deps=[h8_last])
        wst_free = ctx._wst_free2
        wbf_free = ctx._wbf_free4
        bcount = ctx._proj_bcount
        h8_free = dn[-1]
        ln = emit_layernorm(ctx, [dn[-1]] * NT, x_tensor=ctx.scr_x,
                            tile_off=lambda tt: tt * 2048, tiles=range(4))
        for tt in range(4):
            t = s * 4 + tt
            out_toks[t] = out_cb(t, ln[tt])
            slot_free[tt] = out_toks[t]
    ctx._wst_free2 = wst_free
    ctx._wbf_free4 = wbf_free
    ctx._proj_bcount = bcount
    return out_toks



def emit_l0_ffn(ctx, x_deps, tile_cb=None):
    p = ctx.p
    a = ctx.aps
    cc = ctx.cols
    nwb = dma(ctx, "sy", ctx.g_all[:], a["L0nw"].to_broadcast([128, 2048]),
              deps=list(x_deps) if not isinstance(x_deps, list) else [x_deps[-1]])
    tld = dma(ctx, "sy", col(ctx.tcol, 0), a["L0temp"].to_broadcast([128, 1]), deps=[nwb])
    sld = dma(ctx, "sy", col(ctx.sccol, 0), a["L0scale"].to_broadcast([128, 1]), deps=[tld])
    t1 = p.op("ve", lambda e: e.tensor_scalar(col(ctx.tcol, 1), col(ctx.tcol, 0), 0.1, None, ALU.max), deps=[tld])
    t2 = p.op("ve", lambda e: e.reciprocal(col(ctx.tcol, 2), col(ctx.tcol, 1)), deps=[t1])
    gtoks = []
    wst_free0 = getattr(ctx, "_wst_free2", [None, None])
    prev = [sld] + [w for w in wst_free0 if w is not None]
    for quad in range(4):
        ldg = dma(ctx, "sy", ctx.wst[:], a["L0genesP"][quad], deps=prev)
        cg = p.op("ve", lambda e, quad=quad: e.tensor_copy(
            ctx.genes_bf[:, quad * 2048:(quad + 1) * 2048], ctx.wst[:]), deps=[ldg])
        prev = [cg]
        gtoks.append(cg)
    for quad in range(4):
        ldw = dma(ctx, "sy", ctx.wst[:], a["L0wduP"][quad], deps=prev)
        cw = p.op("ve", lambda e, quad=quad: e.tensor_copy(
            ctx.wdu_bf[:, quad * 2048:(quad + 1) * 2048], ctx.wst[:]), deps=[ldw])
        prev = [cw]
        gtoks.append(cw)
    ctx._wst_free2 = [prev[0], prev[0]]

    rms_toks = []
    xnt_toks = []
    prev_xnb = []
    for t in range(NT):
        dep = x_deps[t] if isinstance(x_deps, list) else x_deps
        dd = [dep] if not isinstance(dep, list) else list(dep)
        xa = ctx.xsb[:, t * 2048:(t + 1) * 2048]
        sq = p.op("sc", lambda e, xa=xa: e.activation(ctx.sq_trash[:], xa, AF.Square,
                                                      accum_out=col(cc, C_Q)), deps=dd + rms_toks[-1:])
        v1 = p.op("ve", lambda e: e.tensor_scalar(col(cc, C_VAR), col(cc, C_Q), 1.0 / 2048, None, ALU.mult), deps=[sq])
        l1 = p.op("sc", lambda e: e.activation(col(cc, C_RSTD), col(cc, C_VAR), AF.Ln, bias=col(cc, C_EPS2)), deps=[v1])
        e1 = p.op("sc", lambda e, t=t: e.activation(col(ctx.rs_cols, t), col(cc, C_RSTD), AF.Exp,
                                                    scale=col(cc, C_NHALF)), deps=[l1])
        rms_toks.append(e1)
        r2 = p.op("ve", lambda e, t=t: e.tensor_tensor(col(ctx.rs_cols, 8 + t), col(ctx.rs_cols, t),
                                                       col(ctx.sccol, 0), ALU.mult), deps=[e1, sld])
        rms_toks.append(r2)
        xb = p.op("ve", lambda e, xa=xa, t=t: e.scalar_tensor_tensor(
            ctx.xnb_bf[:], xa, col(ctx.rs_cols, t), ctx.g_all[:], ALU.mult, ALU.mult),
            deps=[e1, nwb] + (prev_xnb if prev_xnb else []))
        last_ev = None
        evs2 = []
        for ci in range(16):
            tr = p.op("te", lambda e, ci=ci: e.transpose(ctx.ps_bf[:, 0:128],
                                                         ctx.xnb_bf[:, ci * 128:(ci + 1) * 128],
                                                         ctx.ident_bf[:]),
                      deps=[xb, ctx.bank_last[7]])
            ev = p.op("sc" if ci % 2 == 0 else "ve",
                      lambda e, t=t, ci=ci: (e.activation(ctx.xT[:, ci * 1024 + t * 128: ci * 1024 + (t + 1) * 128],
                                                          ctx.ps_bf[:, 0:128], AF.Identity)
                                             if ci % 2 == 0 else
                                             e.tensor_copy(ctx.xT[:, ci * 1024 + t * 128: ci * 1024 + (t + 1) * 128],
                                                           ctx.ps_bf[:, 0:128])),
                      deps=[tr])
            ctx.bank_last[7] = ev
            evs2.append(ev)
            last_ev = ev
        prev_xnb = evs2[-2:]
        xnt_toks.append(evs2[-2:])

    stage_free = [None, None]

    def evac_lg(g, sub, psum, deps):
        slot = (g * NT + sub) % 2
        ev = p.op("sc", lambda e, slot=slot, psum=psum: e.activation(
            ctx.lstage[:, slot * 512:(slot + 1) * 512], psum, AF.Identity, scale=col(ctx.tcol, 2)),
            deps=deps + [t2] + ([stage_free[slot]] if stage_free[slot] else []))
        dm = dma(ctx, "gp", a["lg_spill"][sub * 128:(sub + 1) * 128, g * 512:(g + 1) * 512],
                 ctx.lstage[:, slot * 512:(slot + 1) * 512], deps=[ev])
        stage_free[slot] = dm
        return dm

    lg_toks = emit_proj(ctx, a["L0Wc"], 8, 8, 2,
                        lambda ci, sub: ctx.xT[:, ci * 1024 + sub * 128: ci * 1024 + sub * 128 + 128],
                        evac_lg, lhsT_deps=(),
                        lhsT_deps_fn=lambda sub: xnt_toks[sub])

    ln_out = []
    prev_tile = None
    lg_free = None
    mask_free = None
    prev_slot_user = [None] * 32
    for t in range(NT):
        xa = ctx.xsb[:, t * 2048:(t + 1) * 2048]
        rl = dma(ctx, "sy", ctx.lg_f32[:], a["lg_spill"][t * 128:(t + 1) * 128, :],
                 deps=[lg_toks[-1]] + ([lg_free] if lg_free else []))
        wk = dma(ctx, "sy", ctx.work_f32[:], a["lg_spill"][t * 128:(t + 1) * 128, :],
                 deps=[rl] + ([mask_free] if mask_free else []))
        tok = wk
        for rnd in range(4):
            mx = p.op("ve", lambda e, rnd=rnd: e.max(out=ctx.vals32[:, rnd * 8:(rnd + 1) * 8],
                                                     in_=ctx.work_f32[:]), deps=[tok])
            tok = p.op("ve", lambda e, rnd=rnd: e.match_replace(
                out=ctx.work_f32[:], in_to_replace=ctx.vals32[:, rnd * 8:(rnd + 1) * 8],
                in_values=ctx.work_f32[:], imm_value=-1e30), deps=[mx])
        v32 = p.op("ve", lambda e: e.tensor_reduce(col(cc, C_V32), ctx.vals32[:], AX, ALU.min), deps=[tok])
        mcp = p.op("ve", lambda e: e.tensor_copy(col(cc, C_M), ctx.vals32[:, 0:1]), deps=[tok])
        nm = p.op("ve", lambda e: e.tensor_scalar(col(cc, C_NM), col(cc, C_M), -1.0, None, ALU.mult), deps=[mcp])
        ze = p.op("sc", lambda e: e.activation(ctx.vals32[:], ctx.vals32[:], AF.Exp, bias=col(cc, C_NM),
                                               accum_out=col(cc, C_Z)), deps=[nm, tok, v32, mcp])
        lz = p.op("sc", lambda e: e.activation(col(cc, C_ITMP), col(cc, C_Z), AF.Ln), deps=[ze])
        b2a = p.op("ve", lambda e: e.tensor_copy(col(cc, C_B2), col(cc, C_NM)), deps=[lz, nm])
        b2b = p.op("ve", lambda e: e.tensor_sub(col(cc, C_B2), col(cc, C_B2), col(cc, C_ITMP)), deps=[b2a, lz])
        mk_ = p.op("ve", lambda e: e.tensor_scalar(ctx.m_bf[:], ctx.lg_f32[:], col(cc, C_V32), None,
                                                   ALU.is_ge), deps=[v32, tok])
        ep = p.op("sc", lambda e: e.activation(ctx.p_bf[:], ctx.lg_f32[:], AF.Exp, bias=col(cc, C_B2)),
                  deps=[b2b, mk_])
        pp = p.op("ve", lambda e: e.tensor_tensor(ctx.p_bf[:], ctx.p_bf[:], ctx.m_bf[:], ALU.mult),
                  deps=[ep])
        lg_free = ep
        mask_free = pp
        pt_evs = []
        for gc in range(32):
            tb = ctx.ps_bf2[gc % 2]
            tbi = 6 + gc % 2
            tr = p.op("te", lambda e, gc=gc, tb=tb: e.transpose(tb[:, 0:128],
                                                                ctx.p_bf[:, gc * 128:(gc + 1) * 128],
                                                                ctx.ident_bf[:]),
                      deps=[pp, ctx.bank_last[tbi]])
            ctx.bank_last[tbi] = tr
            eng = "sc" if gc % 2 == 0 else "ve"
            if eng == "sc":
                ev = p.op("sc", lambda e, gc=gc, tb=tb: e.activation(
                    ctx.scan_bf[:, gc * 128:(gc + 1) * 128], tb[:, 0:128], AF.Identity),
                    deps=[tr, prev_slot_user[gc]])
            else:
                ev = p.op("ve", lambda e, gc=gc, tb=tb: e.tensor_copy(
                    ctx.scan_bf[:, gc * 128:(gc + 1) * 128], tb[:, 0:128]),
                    deps=[tr, prev_slot_user[gc]])
            ctx.bank_last[tbi] = ev
            pt_evs.append(ev)
        mask_free = pt_evs[-1]
        eh_mm = [None, None]
        for half in range(2):
            bl = ctx.bank_last[4 + half]
            for gc in range(32):
                dep0 = [pt_evs[gc]] + gtoks
                if gc == 0 and bl is not None:
                    dep0.append(bl)
                eh_mm[half] = p.op("te", lambda e, gc=gc, half=half: e.matmul(
                    ctx.ps[4 + half][:, 0:128],
                    ctx.genes_bf[:, gc * 256 + half * 128: gc * 256 + half * 128 + 128],
                    ctx.scan_bf[:, gc * 128:(gc + 1) * 128],
                    start=(gc == 0), stop=(gc == 31)), deps=dep0)
        ex0 = p.op("ve", lambda e: e.tensor_copy(ctx.exprT_bf[:, 0:128], ctx.ps[4][:, 0:128]), deps=[eh_mm[0]])
        ex1 = p.op("sc", lambda e: e.activation(ctx.exprT_bf[:, 128:256], ctx.ps[5][:, 0:128], AF.Identity),
                   deps=[eh_mm[1]])
        prev_slot_user = [eh_mm[1]] * 32
        ctx.bank_last[4] = ex0
        ctx.bank_last[5] = ex1
        dn_evs = []
        for g in range(4):
            bl = ctx.bank_last[g % 2]
            mm1 = p.op("te", lambda e, g=g: e.matmul(ctx.ps[g % 2][:, 0:512], ctx.exprT_bf[:, 0:128],
                                                     ctx.wdu_bf[:, g * 512:(g + 1) * 512],
                                                     start=True, stop=False),
                       deps=[ex0, ex1] + ([bl] if bl else []))
            mm2 = p.op("te", lambda e, g=g: e.matmul(ctx.ps[g % 2][:, 0:512], ctx.exprT_bf[:, 128:256],
                                                     ctx.wdu_bf[:, 2048 + g * 512: 2048 + (g + 1) * 512],
                                                     start=False, stop=True), deps=[mm1])
            dv = p.op("sc", lambda e, g=g: e.activation(ctx.dn_f32[:, g * 512:(g + 1) * 512],
                                                        ctx.ps[g % 2][:, 0:512], AF.Tanh),
                      deps=[mm2, mask_free])
            ctx.bank_last[g % 2] = dv
            dn_evs.append(dv)
        up_evs = []
        for g in range(4):
            bl = ctx.bank_last[2 + g % 2]
            mm1 = p.op("te", lambda e, g=g: e.matmul(ctx.ps[2 + g % 2][:, 0:512], ctx.exprT_bf[:, 0:128],
                                                     ctx.wdu_bf[:, 4096 + g * 512: 4096 + (g + 1) * 512],
                                                     start=True, stop=False),
                       deps=[ex0, ex1] + ([bl] if bl else []))
            mm2 = p.op("te", lambda e, g=g: e.matmul(ctx.ps[2 + g % 2][:, 0:512], ctx.exprT_bf[:, 128:256],
                                                     ctx.wdu_bf[:, 6144 + g * 512: 6144 + (g + 1) * 512],
                                                     start=False, stop=True), deps=[mm1])
            uv = p.op("ve", lambda e, g=g: e.tensor_tensor(ctx.dn_f32[:, g * 512:(g + 1) * 512],
                                                           ctx.ps[2 + g % 2][:, 0:512],
                                                           ctx.dn_f32[:, g * 512:(g + 1) * 512], ALU.mult),
                      deps=[mm2, dn_evs[g]])
            ctx.bank_last[2 + g % 2] = uv
            up_evs.append(uv)
        z2 = p.op("ve", lambda e, xa=xa, t=t: e.scalar_tensor_tensor(
            ctx.dn_f32[:], ctx.dn_f32[:], col(ctx.rs_cols, 8 + t), xa, ALU.mult, ALU.mult),
            deps=up_evs)
        z3 = p.op("gp", lambda e: e.tensor_tensor(ctx.dn_f32[:], ctx.dn_f32[:], ctx.g_all[:], ALU.mult),
                  deps=[z2, nwb])
        ra = p.op("ve", lambda e, xa=xa: e.tensor_tensor(xa, xa, ctx.dn_f32[:], ALU.add), deps=[z3])
        prev_tile = ra
        ln = emit_layernorm(ctx, [[ra] + ln_out[-1:]] * NT, tiles=[t])
        if tile_cb is not None:
            ln_out.append(tile_cb(t, ln[0]))
        else:
            ln_out.append(ln[0])
    ctx._wst_free2 = [prev_tile, prev_tile]
    return ln_out



U_M128, U_M1024, U_JM, U_NJ = range(4)


def emit_attn(ctx, layer, x_deps, x_reload_dram, spill_first, cc_suffix="", pre_xt=None, pre_spills=None):
    p = ctx.p
    a = ctx.aps
    pre = "L0" if layer == 0 else "L1"
    cc_in = "cc_in" + cc_suffix
    cc_out = "cc_out" + cc_suffix
    if pre_xt is not None:
        xt = pre_xt
        spills = pre_spills or []
    else:
        xt = emit_build_xT(ctx, x_deps)
        spills = []
        if spill_first:
            for t in range(NT):
                spills.append(dma(ctx, "gp", x_reload_dram[t * 128:(t + 1) * 128, :],
                                  ctx.xsb[:, t * 2048:(t + 1) * 2048], deps=xt[16 * t + 14: 16 * t + 16]))
    sp_tok = spills[-1] if spills else None

    dc_toks = []
    if layer == 0:
        sld = dma(ctx, "sy", ctx.scol16[:], a["L0slopes"].to_broadcast([128, 16]),
                  deps=ctx.c_ready[-1:])
        m0 = p.op("ve", lambda e: e.memset(col(ctx.ucols, U_M128), -128.0))
        m1 = p.op("ve", lambda e: e.memset(col(ctx.ucols, U_M1024), -1024.0), deps=[m0])
        jm = p.op("ve", lambda e: e.tensor_scalar(col(ctx.ucols, U_JM), ctx.jcol[:], -128.0, None, ALU.add),
                  deps=ctx.c_ready[-1:] + [m1])
        nj = p.op("ve", lambda e: e.tensor_scalar(col(ctx.ucols, U_NJ), ctx.jcol[:], -1.0, None, ALU.mult),
                  deps=[jm])
        ns = p.op("ve", lambda e: e.tensor_scalar(ctx.dcols[:, DC_NEGS:DC_NEGS + 16], ctx.scol16[:], -1.0, None, ALU.mult),
                  deps=[sld, nj])
        ka = p.op("ve", lambda e: e.tensor_tensor(ctx.dcols[:, DC_KDEC:DC_KDEC + 16], ctx.scol16[:],
                                                  col(ctx.ucols, U_JM).to_broadcast([128, 16]), ALU.mult),
                  deps=[sld, ns])
        ke = p.op("sc", lambda e: e.activation(ctx.dcols[:, DC_KDEC:DC_KDEC + 16],
                                               ctx.dcols[:, DC_KDEC:DC_KDEC + 16], AF.Exp), deps=[ka])
        qa = p.op("ve", lambda e: e.tensor_tensor(ctx.dcols[:, DC_QDEC:DC_QDEC + 16], ctx.scol16[:],
                                                  col(ctx.ucols, U_NJ).to_broadcast([128, 16]), ALU.mult),
                  deps=[ka])
        qe = p.op("sc", lambda e: e.activation(ctx.dcols[:, DC_QDEC:DC_QDEC + 16],
                                               ctx.dcols[:, DC_QDEC:DC_QDEC + 16], AF.Exp), deps=[qa, ke])
        be = p.op("sc", lambda e: e.activation(ctx.dcols[:, DC_BD:DC_BD + 16], ctx.scol16[:],
                                               AF.Exp, scale=col(ctx.ucols, U_M128)), deps=[qe, m1, sld])
        dc_toks = [ke, qe, be, ns]

    bias_toks = []
    if layer == 0:
        bias_toks.append(dma(ctx, "sy", ctx.bias_bc[:], a["L0bqkv"].to_broadcast([128, 6144]),
                             deps=ctx.c_ready[-1:]))

    regs = {0: ctx.qreg, 1: ctx.kreg, 2: ctx.vreg}

    def evac_qkv(g, sub, psum, deps):
        reg = regs[g // 4]
        dst = reg[:, sub * 2048 + (g % 4) * 512: sub * 2048 + (g % 4) * 512 + 512]
        if layer == 0:
            return p.op("ve", lambda e, dst=dst, psum=psum, g=g: e.tensor_tensor(
                dst, psum, ctx.bias_bc[:, g * 512:(g + 1) * 512], ALU.add),
                deps=deps + bias_toks + ([sp_tok] if sp_tok else []) + xt[-2:])
        return p.op("sc", lambda e, dst=dst, psum=psum: e.activation(dst, psum, AF.Identity),
                    deps=deps + ([sp_tok] if sp_tok else []) + xt[-2:])

    def rope_reg(reg, prev):
        toks = []
        for t in range(NT):
            cs = t % 2
            cld = dma(ctx, "sy", ctx.cos_bf[:, cs * 128:(cs + 1) * 128],
                      a["cosbf"][t * 128:(t + 1) * 128, :], deps=prev[-1:])
            sldd = dma(ctx, "sy", ctx.sin_bf[:, cs * 128:(cs + 1) * 128],
                       a["sinbf"][t * 128:(t + 1) * 128, :], deps=[cld])
            sl = reg[:, t * 2048:(t + 1) * 2048]
            sl3 = sl.rearrange("p (h j) -> p h j", h=16)
            rot3 = ctx.rope_rot[:].rearrange("p (h j) -> p h j", h=16)
            cosb = bass.AP(ctx.cos_bf, cs * 128, [[256, 128], [0, 16], [1, 128]])
            sinb = bass.AP(ctx.sin_bf, cs * 128, [[256, 128], [0, 16], [1, 128]])
            swap = bass.AP(reg, t * 2048 + 1, [[16384, 128], [2, 1024], [-1, 2]])
            r1 = p.op("ve", lambda e, swap=swap: e.tensor_copy(ctx.rope_rot[:], swap), deps=prev + [sldd])
            r2 = p.op("ve", lambda e, sl3=sl3, cosb=cosb: e.tensor_tensor(sl3, sl3, cosb, ALU.mult), deps=[r1, sldd])
            r3 = p.op("ve", lambda e, rot3=rot3, sinb=sinb: e.tensor_tensor(rot3, rot3, sinb, ALU.mult), deps=[r2])
            r4 = p.op("ve", lambda e, sl=sl: e.tensor_tensor(sl, sl, ctx.rope_rot[:], ALU.add), deps=[r3])
            r5 = p.op("ve", lambda e, sl=sl: e.tensor_scalar_min(ctx.rope_mn[:], sl, 0.0), deps=[r4])
            r6 = p.op("sc", lambda e: e.activation(ctx.rope_rot[:], ctx.rope_mn[:], AF.Exp), deps=[r5])
            r7 = p.op("ve", lambda e, sl=sl: e.scalar_tensor_tensor(sl, sl, 0.0, ctx.rope_rot[:], ALU.max, ALU.add), deps=[r6])
            prev = [r7]
            toks.append(r7)
        return toks

    lhsT_qkv = lambda ci, sub: ctx.xT[:, ci * 1024 + sub * 128: ci * 1024 + sub * 128 + 128]
    xt_by_tile = [xt[16 * t + 14: 16 * t + 16] for t in range(NT)]
    kv = emit_proj(ctx, a[pre + "Wqkv"], 12, 8, 2, lhsT_qkv, evac_qkv,
                   g_list=range(4, 12), lhsT_deps=(),
                   lhsT_deps_fn=lambda sub: xt_by_tile[sub],
                   cast_eng="sc" if layer == 1 else "ve")

    rope_done = kv[-1]
    if layer == 1:
        rope_done = rope_reg(ctx.kreg, [kv[-1]])[-1]

    gz = p.op("ve", lambda e: e.memset(ctx.g_all[:], 0.0), deps=[rope_done] + dc_toks)
    zz = p.op("ve", lambda e: e.memset(ctx.z_all[:], 0.0), deps=[gz])
    cc_dmas = []
    last_s = {}
    SLOT = lambda pi, k: ctx.scan_bf[:, (pi * 8 + k) * 128: (pi * 8 + k) * 128 + 128]
    pv1 = [{"kd": None} for _ in range(4)]
    st_tok_h = {}
    z_tok_h = {}
    for hg in range(NH // 4):
        heads = [hg * 4 + i for i in range(4)]
        for h in heads:
            st_tok_h[h] = zz
            z_tok_h[h] = zz
        for c in range(NCH):
            ctxv = []
            for pi, h in enumerate(heads):
                gh = ctx.g_all[:, h * 128:(h + 1) * 128]
                ks = ctx.kreg[:, c * 2048 + h * 128: c * 2048 + h * 128 + 128]
                vs = ctx.vreg[:, c * 2048 + h * 128: c * 2048 + h * 128 + 128]
                abank = ctx.ps[3 + 2 * (pi % 2)][:, 0:128]
                ctxv.append((h, gh, ks, vs, abank, 3 + 2 * (pi % 2)))
            if layer == 0:
                kds = []
                for pi, (h, gh, ks, vs, abank, bidx) in enumerate(ctxv):
                    kds.append(p.op("sc", lambda e, ks=ks, h=h, pi=pi: e.activation(
                        SLOT(pi, 3), ks, AF.Identity, scale=col(ctx.dcols, DC_KDEC + h)),
                        deps=[rope_done] + dc_toks + [st_tok_h[h], pv1[pi]["kd"]]))
                for wave in range(2):
                    As = {}
                    for pi in (2 * wave, 2 * wave + 1):
                        h, gh, ks, vs, abank, bidx = ctxv[pi]
                        A = p.op("te", lambda e, vs=vs, pi=pi, abank=abank: e.matmul(
                            abank, SLOT(pi, 3), vs, start=True, stop=True),
                            deps=[kds[pi], ctx.bank_last[bidx]])
                        pv1[pi]["kd"] = A
                        As[pi] = A
                        ctx.bank_last[bidx] = A
                    for pi in (2 * wave, 2 * wave + 1):
                        h, gh, ks, vs, abank, bidx = ctxv[pi]
                        st_tok_h[h] = p.op("ve", lambda e, gh=gh, h=h, abank=abank: e.scalar_tensor_tensor(
                            gh, gh, col(ctx.dcols, DC_BD + h), abank, ALU.mult, ALU.add),
                            deps=[As[pi], st_tok_h[h]])
                        ctx.bank_last[bidx] = st_tok_h[h]
            else:
                for wave in range(2):
                    As = {}
                    for pi in (2 * wave, 2 * wave + 1):
                        h, gh, ks, vs, abank, bidx = ctxv[pi]
                        As[pi] = p.op("te", lambda e, ks=ks, vs=vs, abank=abank: e.matmul(
                            abank, ks, vs, start=True, stop=True),
                            deps=[rope_done, st_tok_h[h], ctx.bank_last[bidx]])
                        ctx.bank_last[bidx] = As[pi]
                    for pi in (2 * wave, 2 * wave + 1):
                        h, gh, ks, vs, abank, bidx = ctxv[pi]
                        st_tok_h[h] = p.op("ve", lambda e, gh=gh, abank=abank: e.tensor_tensor(
                            gh, gh, abank, ALU.add), deps=[As[pi], st_tok_h[h]])
                        ctx.bank_last[bidx] = st_tok_h[h]
                for wave in range(2):
                    Zps = {}
                    for pi in (2 * wave, 2 * wave + 1):
                        h, gh, ks, vs, abank, bidx = ctxv[pi]
                        zb = ctx.ps[pi % 2][:, 0:1]
                        Zps[pi] = p.op("te", lambda e, ks=ks, zb=zb: e.matmul(zb, ks, ctx.onecol_bf[:],
                                                                              start=True, stop=True),
                                       deps=[rope_done, ctx.bank_last[pi % 2], z_tok_h[h]])
                        ctx.bank_last[pi % 2] = Zps[pi]
                    for pi in (2 * wave, 2 * wave + 1):
                        h, gh, ks, vs, abank, bidx = ctxv[pi]
                        zb = ctx.ps[pi % 2][:, 0:1]
                        z_tok_h[h] = p.op("ve", lambda e, h=h, zb=zb: e.tensor_tensor(
                            col(ctx.z_all, h), col(ctx.z_all, h), zb, ALU.add),
                            deps=[Zps[pi], z_tok_h[h]])
                        ctx.bank_last[pi % 2] = z_tok_h[h]
        for h in heads:
            last_s[h] = st_tok_h[h]
            gh = ctx.g_all[:, h * 128:(h + 1) * 128]
            cc_dmas.append(dma(ctx, "gp", a[cc_in][h * 128:(h + 1) * 128, :], gh, deps=[st_tok_h[h]]))
            if layer == 1:
                ztr = p.op("te", lambda e, h=h: e.transpose(ctx.ps[4][0:1, 0:128], col(ctx.z_all, h),
                                                            ctx.ident_f32[:]),
                           deps=[z_tok_h[h], ctx.bank_last[4]])
                zev = p.op("sc", lambda e: e.activation(ctx.zrow[:], ctx.ps[4][0:1, 0:128], AF.Identity),
                           deps=[ztr])
                ctx.bank_last[4] = zev
                cc_dmas.append(dma(ctx, "gp", a[cc_in][2048 + h:2048 + h + 1, :], ctx.zrow[:], deps=[zev]))
            else:
                cc_dmas.append(dma(ctx, "gp", a[cc_in][2048 + h:2048 + h + 1, :], ctx.zrow[:],
                                   deps=[st_tok_h[h]] + ctx.c_ready[-1:]))


    cc = p.op("gp", lambda e: e.collective_compute(
        "AllGather", ALU.bypass, replica_groups=[list(range(NCORES))],
        ins=[a[cc_in].opt()], outs=[a[cc_out].opt()]), deps=cc_dmas, solo=True)

    qp = emit_proj(ctx, a[pre + "Wqkv"], 12, 8, 2, lhsT_qkv, evac_qkv,
                   g_list=range(0, 4), lhsT_deps=(),
                   lhsT_deps_fn=lambda sub: xt_by_tile[sub])
    q_done = qp[-1]
    if layer == 1:
        q_done = rope_reg(ctx.qreg, [qp[-1]])[-1]

    fm_toks = []
    if layer == 0:
        fm = []
        for h in range(NH):
            fm.append(p.op("ve", lambda e, h=h: e.tensor_tensor(
                ctx.fmat[:, h * 8:(h + 1) * 8], col(ctx.scol16, h).to_broadcast([128, 8]),
                ctx.dist8[:], ALU.mult), deps=[cc_dmas[-1]] + dc_toks + (fm[-1:] if fm else [])))
        fe = p.op("sc", lambda e: e.activation(ctx.fmat[:], ctx.fmat[:], AF.Exp,
                                               scale=col(ctx.ucols, U_M1024)), deps=[fm[-1]])
        pmb = bass.AP(ctx.pm8, 0, [[8, 128], [0, 16], [1, 8]])
        fm3 = ctx.fmat[:].rearrange("p (h j) -> p h j", h=16)
        fp = p.op("ve", lambda e, fm3=fm3, pmb=pmb: e.tensor_tensor(fm3, fm3, pmb, ALU.mult), deps=[fe])
        fm_toks = [fp]
    gzero = p.op("ve", lambda e: e.memset(ctx.g_all[:], 0.0), deps=[cc_dmas[-1]] + [last_s[NH - 1]])
    zzero = p.op("ve", lambda e: e.memset(ctx.z_all[:], 0.0), deps=[gzero]) if layer == 1 else gzero
    comb_tok = zzero
    wst_free0 = getattr(ctx, "_wst_free2", [None, None])
    wfl = [w for w in wst_free0 if w is not None]
    for cp in range(NCORES):
        gl = []
        for qh in range(4):
            gl.append(dma(ctx, "sy", ctx.wst[:, qh * 512:(qh + 1) * 512].rearrange("p (h j) -> p h j", h=4),
                          a[cc_out][cp * 2064 + qh * 512: cp * 2064 + (qh + 1) * 512, :]
                          .rearrange("(h p) j -> p h j", p=128),
                          deps=[cc, comb_tok] + wfl))
        wfl = []
        for h in range(NH):
            if layer == 0:
                fcol = ctx.fmat[:, h * 8 + cp: h * 8 + cp + 1]
            else:
                fcol = col(ctx.pm8, cp)
            comb_tok = p.op("ve", lambda e, h=h, fcol=fcol: e.scalar_tensor_tensor(
                ctx.g_all[:, h * 128:(h + 1) * 128], ctx.wst[:, h * 128:(h + 1) * 128],
                fcol, ctx.g_all[:, h * 128:(h + 1) * 128], ALU.mult, ALU.add),
                deps=[gl[h // 4]] + fm_toks + [comb_tok])
        wst_free = comb_tok
    ctx._wst_free2 = [comb_tok, comb_tok]
    if layer == 1:
        for cp in range(NCORES):
            zld = dma(ctx, "sy", ctx.scrT_f32[0:16, 0:128],
                      a[cc_out][cp * 2064 + 2048: cp * 2064 + 2064, :], deps=[cc, comb_tok, q_done])
            ztr = p.op("te", lambda e: e.transpose(ctx.ps[4][0:16, 0:16].transpose([0, 1]) if False else ctx.ps[4][:, 0:16],
                                                   ctx.scrT_f32[0:16, 0:128], ctx.ident_f32[0:16, 0:16]),
                       deps=[zld, ctx.bank_last[4]])
            comb_tok = p.op("ve", lambda e, cp=cp: e.scalar_tensor_tensor(
                ctx.z_all[:], ctx.ps[4][:, 0:16], col(ctx.pm8, cp), ctx.z_all[:], ALU.mult, ALU.add),
                deps=[ztr, comb_tok, zzero])
            ctx.bank_last[4] = comb_tok

    comb_tok = p.op("ve", lambda e: e.engine_nop() if False else e.nop(), deps=[comb_tok, q_done])

    comb_tok = p.op("ve", lambda e: e.nop(), deps=[comb_tok, q_done])

    oT_done = []
    SLOT = lambda pi, k: ctx.scan_bf[:, (pi * 8 + k) * 128: (pi * 8 + k) * 128 + 128]
    S_QT, S_KT, S_QDT, S_KD, S_QD, S_SD, S_SB, S_X = range(8)
    pv = [{k: None for k in ("qT", "kT", "qdT", "sd", "kd", "qd", "sbf", "zc", "dbf", "rep", "intra")}
          for _ in range(4)]
    den_sl = lambda pi: ctx.den_sb[0:1, pi * 256: pi * 256 + 128]
    deni_sl = lambda pi: ctx.den_sb[0:1, pi * 256 + 128: pi * 256 + 256]
    z_tok2 = {h: comb_tok for h in range(NH)}
    s_tok2 = {h: comb_tok for h in range(NH)}
    last_oe = None

    def tr_pair(pis, srcs, dst_k, deps_map, war_key):
        ts_ = {}
        for pi in pis:
            tb = ctx.ps_bf2[pi % 2]
            tbi = 6 + pi % 2
            ts_[pi] = p.op("te", lambda e, tb=tb, src=srcs[pi]: e.transpose(tb[:, 0:128], src,
                                                                            ctx.ident_bf[:]),
                           deps=list(deps_map[pi]) + [ctx.bank_last[tbi]])
            ctx.bank_last[tbi] = ts_[pi]
        evs = {}
        for j, pi in enumerate(pis):
            tb = ctx.ps_bf2[pi % 2]
            tbi = 6 + pi % 2
            if j % 2 == 0:
                evs[pi] = p.op("sc", lambda e, pi=pi, tb=tb, dst_k=dst_k: e.activation(
                    SLOT(pi, dst_k), tb[:, 0:128], AF.Identity), deps=[ts_[pi], pv[pi][war_key]])
            else:
                evs[pi] = p.op("ve", lambda e, pi=pi, tb=tb, dst_k=dst_k: e.tensor_copy(
                    SLOT(pi, dst_k), tb[:, 0:128]), deps=[ts_[pi], pv[pi][war_key]])
            ctx.bank_last[tbi] = evs[pi]
        return evs

    for hg in range(NH // 4):
        heads = [hg * 4 + i for i in range(4)]
        ih = {pi: [] for pi in range(4)}
        qdf = {}
        if layer == 0:
            i1s = {}
            for pi, h in enumerate(heads):
                i1s[pi] = p.op("sc", lambda e, h=h, pi=pi: e.activation(
                    SLOT(pi, S_X), ctx.delta_bf[:], AF.Exp, scale=col(ctx.dcols, DC_NEGS + h)),
                    deps=[comb_tok] + dc_toks + [pv[pi]["intra"]])
            for pi, h in enumerate(heads):
                qdf[pi] = p.op("sc", lambda e, h=h, pi=pi: e.activation(
                    SLOT(pi, S_QD), ctx.iota_bf[:], AF.Exp, scale=col(ctx.dcols, DC_NEGS + h)),
                    deps=[comb_tok] + dc_toks + [pv[pi]["qd"]])
            for pi, h in enumerate(heads):
                i2_ = p.op("ve", lambda e, pi=pi: e.tensor_tensor(SLOT(pi, S_X), SLOT(pi, S_X),
                                                                  ctx.trilT_bf[:], ALU.mult),
                           deps=[i1s[pi]])
                ih[pi] = [i2_]
        for c in range(NCH):
            for pp in range(2):
                pis = [2 * pp, 2 * pp + 1]
                H = {}
                for pi in pis:
                    h = heads[pi]
                    H[pi] = (h,
                             ctx.g_all[:, h * 128:(h + 1) * 128],
                             ctx.qreg[:, c * 2048 + h * 128: c * 2048 + h * 128 + 128],
                             ctx.kreg[:, c * 2048 + h * 128: c * 2048 + h * 128 + 128],
                             ctx.vreg[:, c * 2048 + h * 128: c * 2048 + h * 128 + 128])
                e1s = tr_pair(pis, {pi: H[pi][2] for pi in pis}, S_QT,
                              {pi: [comb_tok] for pi in pis}, "qT")
                e2s = tr_pair(pis, {pi: H[pi][3] for pi in pis}, S_KT,
                              {pi: [comb_tok] for pi in pis}, "kT")
                if layer == 0:
                    e3s = {}
                    for pi in pis:
                        e3s[pi] = p.op("ve", lambda e, pi=pi: e.tensor_tensor(
                            SLOT(pi, S_QDT), SLOT(pi, S_QT), SLOT(pi, S_QD), ALU.mult),
                            deps=[e1s[pi], qdf[pi], pv[pi]["qdT"]])
                        pv[pi]["qd"] = e3s[pi]
                sts, sds = {}, {}
                for pi in pis:
                    sb_ = 4 if pi % 2 == 0 else 2
                    sts[pi] = p.op("te", lambda e, pi=pi, sb_=sb_: e.matmul(
                        ctx.ps[sb_][:, 0:128], SLOT(pi, S_KT), SLOT(pi, S_QT), start=True, stop=True),
                        deps=[e1s[pi], e2s[pi], ctx.bank_last[sb_]])
                    ctx.bank_last[sb_] = sts[pi]
                for pi in pis:
                    sb_ = 4 if pi % 2 == 0 else 2
                    mask = SLOT(pi, S_X) if layer == 0 else ctx.trilT_bf[:]
                    sds[pi] = p.op("ve", lambda e, pi=pi, sb_=sb_, mask=mask: e.tensor_tensor(
                        SLOT(pi, S_SD), ctx.ps[sb_][:, 0:128], mask, ALU.mult),
                        deps=[sts[pi], pv[pi]["sd"]] + (ih[pi] if layer == 0 else []))
                    if layer == 0:
                        pv[pi]["intra"] = sds[pi]
                    ctx.bank_last[sb_] = sds[pi]
                rpes, d2s, zcs = {}, {}, {}
                if layer == 1:
                    for pi in pis:
                        h = heads[pi]
                        zcs[pi] = p.op("ve", lambda e, h=h, pi=pi: e.tensor_copy(
                            col(ctx.zcol_bf, pi), col(ctx.z_all, h)), deps=[z_tok2[h], pv[pi]["zc"]])
                    for pi in pis:
                        db = pi % 2
                        d1 = p.op("te", lambda e, pi=pi, db=db: e.matmul(
                            ctx.ps[db][0:1, 0:128], ctx.onecol_bf[:], SLOT(pi, S_SD),
                            start=True, stop=False), deps=[sds[pi], ctx.bank_last[db]])
                        d2s[pi] = p.op("te", lambda e, pi=pi, db=db: e.matmul(
                            ctx.ps[db][0:1, 0:128], col(ctx.zcol_bf, pi), SLOT(pi, S_QT),
                            start=False, stop=True), deps=[zcs[pi], e1s[pi], d1])
                        pv[pi]["zc"] = d2s[pi]
                        ctx.bank_last[db] = d2s[pi]
                    d5s = {}
                    for pi in pis:
                        db = pi % 2
                        d3 = p.op("ve", lambda e, pi=pi, db=db: e.tensor_scalar_max(
                            den_sl(pi), ctx.ps[db][0:1, 0:128], 1e-6), deps=[d2s[pi]])
                        d4 = p.op("ve", lambda e, pi=pi: e.reciprocal(deni_sl(pi), den_sl(pi)), deps=[d3])
                        d5s[pi] = p.op("ve", lambda e, pi=pi: e.tensor_copy(
                            ctx.den_bf[0:1, pi * 128:(pi + 1) * 128], deni_sl(pi)),
                            deps=[d4, pv[pi]["dbf"]])
                        ctx.bank_last[db] = d5s[pi]
                    rps = {}
                    for pi in pis:
                        db = pi % 2
                        rps[pi] = p.op("te", lambda e, pi=pi, db=db: e.matmul(
                            ctx.ps[db][:, 128:256], ctx.ones_bf[0:1, 0:128],
                            ctx.den_bf[0:1, pi * 128:(pi + 1) * 128], start=True, stop=True),
                            deps=[d5s[pi], ctx.bank_last[db]])
                        pv[pi]["dbf"] = rps[pi]
                        ctx.bank_last[db] = rps[pi]
                    for pi in pis:
                        db = pi % 2
                        rpes[pi] = p.op("sc", lambda e, pi=pi, db=db: e.activation(
                            SLOT(pi, S_X), ctx.ps[db][:, 128:256], AF.Identity),
                            deps=[rps[pi], pv[pi]["rep"]])
                        ctx.bank_last[db] = rpes[pi]
                scsts = {}
                for pi in pis:
                    h = heads[pi]
                    scsts[pi] = p.op("ve", lambda e, gh=H[pi][1], pi=pi: e.tensor_copy(SLOT(pi, S_SB), gh),
                                     deps=[s_tok2[h], pv[pi]["sbf"]])
                o2s = {}
                for pi in pis:
                    ob = 5 if pi % 2 == 0 else 3
                    o1 = p.op("te", lambda e, vs=H[pi][4], pi=pi, ob=ob: e.matmul(
                        ctx.ps[ob][:, 0:128], vs, SLOT(pi, S_SD), start=True, stop=False),
                        deps=[sds[pi], ctx.bank_last[ob]])
                    if layer == 0:
                        o2s[pi] = p.op("te", lambda e, pi=pi, ob=ob: e.matmul(
                            ctx.ps[ob][:, 0:128], SLOT(pi, S_SB), SLOT(pi, S_QDT), start=False, stop=True),
                            deps=[o1, scsts[pi], e3s[pi]])
                        pv[pi]["qdT"] = o2s[pi]
                    else:
                        o2s[pi] = p.op("te", lambda e, pi=pi, ob=ob: e.matmul(
                            ctx.ps[ob][:, 0:128], SLOT(pi, S_SB), SLOT(pi, S_QT), start=False, stop=True),
                            deps=[o1, scsts[pi], e1s[pi]])
                    pv[pi]["sd"] = o1
                    pv[pi]["sbf"] = o2s[pi]
                    pv[pi]["qT"] = sts[pi] if layer == 0 else o2s[pi]
                    pv[pi]["kT"] = sts[pi]
                    ctx.bank_last[ob] = o2s[pi]
                for pi in pis:
                    h = heads[pi]
                    ob = 5 if pi % 2 == 0 else 3
                    od = ctx.oT[:, h * 1024 + c * 128: h * 1024 + (c + 1) * 128]
                    if layer == 0:
                        oe = p.op("sc", lambda e, od=od, ob=ob: e.activation(
                            od, ctx.ps[ob][:, 0:128], AF.Identity), deps=[o2s[pi]])
                    else:
                        oe = p.op("ve", lambda e, od=od, ob=ob, pi=pi: e.tensor_tensor(
                            od, ctx.ps[ob][:, 0:128], SLOT(pi, S_X), ALU.mult),
                            deps=[o2s[pi], rpes[pi]])
                        pv[pi]["rep"] = oe
                        pv[pi]["intra"] = oe
                    ctx.bank_last[ob] = oe
                    oT_done.append(oe)
                    last_oe = oe
                if layer == 0:
                    kds2, Amms = {}, {}
                    for pi in pis:
                        h = heads[pi]
                        kds2[pi] = p.op("sc", lambda e, ks=H[pi][3], h=h, pi=pi: e.activation(
                            SLOT(pi, S_KD), ks, AF.Identity, scale=col(ctx.dcols, DC_KDEC + h)),
                            deps=[comb_tok] + dc_toks + [pv[pi]["kd"]])
                    for pi in pis:
                        ab = 5 if pi % 2 == 0 else 3
                        Amms[pi] = p.op("te", lambda e, vs=H[pi][4], pi=pi, ab=ab: e.matmul(
                            ctx.ps[ab][:, 128:256], SLOT(pi, S_KD), vs, start=True, stop=True),
                            deps=[kds2[pi], ctx.bank_last[ab]])
                        pv[pi]["kd"] = Amms[pi]
                        ctx.bank_last[ab] = Amms[pi]
                    for pi in pis:
                        h = heads[pi]
                        ab = 5 if pi % 2 == 0 else 3
                        s_tok2[h] = p.op("ve", lambda e, gh=H[pi][1], h=h, ab=ab: e.scalar_tensor_tensor(
                            gh, gh, col(ctx.dcols, DC_BD + h), ctx.ps[ab][:, 128:256], ALU.mult, ALU.add),
                            deps=[Amms[pi], scsts[pi], o2s[pi], s_tok2[h]])
                        ctx.bank_last[ab] = s_tok2[h]
                else:
                    Amms, Zmms = {}, {}
                    for pi in pis:
                        ab = 5 if pi % 2 == 0 else 3
                        Amms[pi] = p.op("te", lambda e, ks=H[pi][3], vs=H[pi][4], ab=ab: e.matmul(
                            ctx.ps[ab][:, 128:256], ks, vs, start=True, stop=True),
                            deps=[ctx.bank_last[ab], sds[pi]])
                        ctx.bank_last[ab] = Amms[pi]
                    for pi in pis:
                        h = heads[pi]
                        ab = 5 if pi % 2 == 0 else 3
                        s_tok2[h] = p.op("ve", lambda e, gh=H[pi][1], ab=ab: e.tensor_tensor(
                            gh, gh, ctx.ps[ab][:, 128:256], ALU.add),
                            deps=[Amms[pi], scsts[pi], o2s[pi], s_tok2[h]])
                        ctx.bank_last[ab] = s_tok2[h]
                    for pi in pis:
                        ab = 5 if pi % 2 == 0 else 3
                        Zmms[pi] = p.op("te", lambda e, ks=H[pi][3], ab=ab: e.matmul(
                            ctx.ps[ab][:, 384:385], ks, ctx.onecol_bf[:], start=True, stop=True),
                            deps=[ctx.bank_last[ab], d2s[pi]])
                        ctx.bank_last[ab] = Zmms[pi]
                    for pi in pis:
                        h = heads[pi]
                        ab = 5 if pi % 2 == 0 else 3
                        z_tok2[h] = p.op("ve", lambda e, h=h, ab=ab: e.tensor_tensor(
                            col(ctx.z_all, h), col(ctx.z_all, h), ctx.ps[ab][:, 384:385], ALU.add),
                            deps=[Zmms[pi], zcs[pi], z_tok2[h]])
                        ctx.bank_last[ab] = z_tok2[h]
    scan_done = last_oe
    s_tok = s_tok2[NH - 1]
    z_tok = z_tok2[NH - 1] if layer == 1 else s_tok

    xrl = []
    for t in range(NT):
        xrl.append(dma(ctx, "gp", ctx.xsb[:, t * 2048:(t + 1) * 2048],
                       x_reload_dram[t * 128:(t + 1) * 128, :],
                       deps=[scan_done, s_tok, z_tok]))
    badd = []
    if layer == 0:
        bld = dma(ctx, "sy", ctx.g_all[:], a["L0bout"].to_broadcast([128, 2048]),
                  deps=[scan_done, s_tok, z_tok])
        for t in range(NT):
            badd.append(p.op("ve", lambda e, t=t: e.tensor_tensor(
                ctx.xsb[:, t * 2048:(t + 1) * 2048], ctx.xsb[:, t * 2048:(t + 1) * 2048],
                ctx.g_all[:], ALU.add), deps=[xrl[t], bld] + (badd[-1:] if badd else [])))
    xready = badd if badd else xrl

    def evac_out(g, sub, psum, deps):
        dst = ctx.xsb[:, sub * 2048 + g * 512: sub * 2048 + (g + 1) * 512]
        return p.op("ve", lambda e, dst=dst, psum=psum: e.tensor_tensor(dst, psum, dst, ALU.add),
                    deps=deps + [xready[sub] if isinstance(xready, list) else xready])

    op_toks = emit_proj(ctx, a[pre + "Wout"], 4, 8, 2,
                        lambda ci, sub: ctx.oT[:, ci * 1024 + sub * 128: ci * 1024 + sub * 128 + 128],
                        evac_out, lhsT_deps=[scan_done])
    per_tile = {}
    for i, tok in enumerate(op_toks):
        g = i // NT
        sub = i % NT
        per_tile[sub] = tok
    ln = emit_layernorm(ctx, [per_tile[t] for t in range(NT)])
    return ln



def pack_proj_w(w, n_groups, nblk, ci_per_blk):
    K, n_out = w.shape
    assert K == nblk * ci_per_blk * 128 and n_out == n_groups * 512
    r = w.reshape(nblk, ci_per_blk, 128, n_groups, 512)
    return np.ascontiguousarray(r.transpose(3, 0, 2, 1, 4).reshape(n_groups, nblk, 128, ci_per_blk * 512))


def pack_colblocks(w):
    K, n_out = w.shape
    r = w.reshape(16, 128, n_out // 128, 128)
    return np.ascontiguousarray(r.transpose(2, 1, 0, 3).reshape(n_out // 128, 128, 16 * 128))


def host_consts():
    j = np.arange(128, dtype=np.float32)
    ident = np.eye(128, dtype=np.float32)
    delta = (j[None, :] - j[:, None]).astype(np.float32)
    trilT = (delta >= 0).astype(np.float32)
    delta = delta * trilT
    bf = ml_dtypes.bfloat16
    iota = np.broadcast_to(j[None, :], (128, 128)).copy()
    return {
        "c_ident": ident,
        "c_ident_bf": ident.astype(bf),
        "c_ones_bf": np.ones((128, 128), bf),
        "c_delta_bf": delta.astype(bf),
        "c_trilT_bf": trilT.astype(bf),
        "c_iota_bf": iota.astype(bf),
        "c_jcol": j.reshape(128, 1).copy(),
    }


def core_consts(core):
    pm = np.zeros((1, 8), np.float32)
    pm[0, :core] = 1.0
    dist = np.maximum(core - 1 - np.arange(8, dtype=np.float32), 0.0).reshape(1, 8)
    return {"c_pmask": pm, "c_dist": dist}


CONST_DECLS = {
    "c_ident": ([128, 128], F32), "c_ident_bf": ([128, 128], BF16),
    "c_ones_bf": ([128, 128], BF16), "c_delta_bf": ([128, 128], BF16),
    "c_trilT_bf": ([128, 128], BF16), "c_jcol": ([128, 1], F32),
    "c_iota_bf": ([128, 128], BF16),
    "c_pmask": ([1, 8], F32), "c_dist": ([1, 8], F32),
}


def declare(nc, name, shape, dtype, kind):
    if kind == "Internal":
        return nc.dram_tensor(name, list(shape), dtype).ap()
    return nc.dram_tensor(name, list(shape), dtype, kind=kind).ap()


def new_ctx(extra_decls):
    ctx = Ctx()
    ctx.nc = bass.Bass("TRN2", target_bir_lowering=False)
    ctx.p = Prog()
    ctx.aps = {}
    for n, (shape, dt_) in CONST_DECLS.items():
        ctx.aps[n] = declare(ctx.nc, n, shape, dt_, "ExternalInput")
    for n, (shape, dt_, kind) in extra_decls.items():
        ctx.aps[n] = declare(ctx.nc, n, shape, dt_, kind)
    alloc_sbuf(ctx)
    return ctx



def build_l1_ffn_block():
    decls = {
        "x_in": ([T, D], F32, "ExternalInput"),
        "L1g": ([128, 128, 2048], F32, "ExternalInput"),
        "L1d": ([4, 32, 128, 1024], F32, "ExternalInput"),
        "x_out": ([T, D], F32, "ExternalOutput"),
        "x_spill": ([T, D], F32, "Internal"),
    }
    ctx = new_ctx(decls)
    emit_consts(ctx)
    xl = emit_load_x(ctx, ctx.aps["x_in"])

    def out_cb(t, tok):
        return dma(ctx, "gp", ctx.aps["x_out"][t * 128:(t + 1) * 128, :],
                   ctx.scr_x[:, (t % 4) * 2048:((t % 4) + 1) * 2048], deps=[tok])

    st = emit_l1_ffn(ctx, xl, out_cb)
    ctx.p.op("gp", lambda e: None, deps=st)
    ctx.p.emit(ctx.nc)
    return ctx


def run_l1_ffn_standalone(x_full, w_gate_up, w_down):
    ctx = build_l1_ffn_block()
    hc = host_consts()
    l1g = pack_colblocks(np.ascontiguousarray(w_gate_up.T))
    l1d = pack_proj_w(np.ascontiguousarray(w_down.T), 4, 32, 2)
    x2 = x_full.reshape(SEQ, D)
    in_maps = []
    for c in range(NCORES):
        m = dict(hc)
        m.update(core_consts(c))
        m["x_in"] = np.ascontiguousarray(x2[c * T:(c + 1) * T])
        m["L1g"] = l1g
        m["L1d"] = l1d
        in_maps.append(m)
    res = run_bass_kernel_spmd(ctx.nc, in_maps, core_ids=list(range(NCORES)))
    return np.concatenate([res.results[c]["x_out"] for c in range(NCORES)], 0).reshape(1, SEQ, D)


SEQ = 8192


def make_rope_tables():
    inv = 1.0 / (10000.0 ** (np.arange(0, 128, 2, dtype=np.float64) / 128.0))
    pos = np.arange(8192, dtype=np.float64)
    fr = np.outer(pos, inv)
    emb = np.concatenate([fr, fr], -1)
    cos = np.cos(emb)
    sin = np.sin(emb)
    alt = np.ones(128)
    alt[0::2] = -1.0
    sinalt = sin * alt[None, :]
    bf = ml_dtypes.bfloat16
    return cos.astype(bf), sinalt.astype(bf)


def build_attn_block(layer):
    decls = {
        "x_in": ([T, D], F32, "ExternalInput"),
        "x_out": ([T, D], F32, "ExternalOutput"),
        "cc_in": ([2064, 128], F32, "Internal"),
        "cc_out": ([8 * 2064, 128], F32, "Internal"),
    }
    if layer == 0:
        decls.update({
            "L0Wqkv": ([12, 8, 128, 1024], F32, "ExternalInput"),
            "L0Wout": ([4, 8, 128, 1024], F32, "ExternalInput"),
            "L0bqkv": ([1, 6144], F32, "ExternalInput"),
            "L0bout": ([1, 2048], F32, "ExternalInput"),
            "L0slopes": ([1, 16], F32, "ExternalInput"),
        })
    else:
        decls.update({
            "L1Wqkv": ([12, 8, 128, 1024], F32, "ExternalInput"),
            "L1Wout": ([4, 8, 128, 1024], F32, "ExternalInput"),
            "cosbf": ([1024, 128], BF16, "ExternalInput"),
            "sinbf": ([1024, 128], BF16, "ExternalInput"),
            "x_spill": ([T, D], F32, "Internal"),
        })
    ctx = new_ctx(decls)
    emit_consts(ctx)
    xl = emit_load_x(ctx, ctx.aps["x_in"])
    reload_dram = ctx.aps["x_in"] if layer == 0 else ctx.aps["x_spill"]
    ln = emit_attn(ctx, layer, xl, reload_dram, spill_first=(layer == 1))
    st = emit_store_x(ctx, ctx.aps["x_out"], ln)
    ctx.p.op("gp", lambda e: None, deps=st)
    ctx.p.emit(ctx.nc)
    return ctx


def run_attn_standalone(layer, x_full, inputs):
    ctx = build_attn_block(layer)
    hc = host_consts()
    x2 = x_full.reshape(SEQ, D)
    shared = {}
    if layer == 0:
        shared["L0Wqkv"] = pack_proj_w(inputs["L0_attn_Wqkv"], 12, 4, 4)
        shared["L0Wout"] = pack_proj_w(inputs["L0_attn_Wout"], 4, 4, 4)
        shared["L0bqkv"] = inputs["L0_attn_bqkv"].reshape(1, 6144)
        shared["L0bout"] = inputs["L0_attn_bout"].reshape(1, 2048)
        shared["L0slopes"] = inputs["L0_attn_slopes"].reshape(1, 16)
    else:
        shared["L1Wqkv"] = pack_proj_w(np.ascontiguousarray(inputs["L1_attn_W_qkv"].T), 12, 8, 2)
        shared["L1Wout"] = pack_proj_w(np.ascontiguousarray(inputs["L1_attn_W_out"].T), 4, 8, 2)
        cosb, sinb = make_rope_tables()
        shared["_cos_full"] = cosb
        shared["_sin_full"] = sinb
    in_maps = []
    for c in range(NCORES):
        m = dict(hc)
        m.update(core_consts(c))
        m["x_in"] = np.ascontiguousarray(x2[c * T:(c + 1) * T])
        for k, v in shared.items():
            if k == "_cos_full":
                m["cosbf"] = np.ascontiguousarray(v[c * T:(c + 1) * T])
            elif k == "_sin_full":
                m["sinbf"] = np.ascontiguousarray(v[c * T:(c + 1) * T])
            else:
                m[k] = v
        in_maps.append(m)
    res = run_bass_kernel_spmd(ctx.nc, in_maps, core_ids=list(range(NCORES)))
    return np.concatenate([res.results[c]["x_out"] for c in range(NCORES)], 0).reshape(1, SEQ, D)


def pack_genes(genes):
    img = genes.reshape(32, 128, 256).transpose(1, 0, 2).reshape(128, 8192)
    return np.ascontiguousarray(img.reshape(128, 4, 2048).transpose(1, 0, 2))


def pack_wdu(Wd, Wu):
    def img(w):
        wT = np.ascontiguousarray(w.T)
        return wT.reshape(2, 128, 2048).transpose(1, 0, 2).reshape(128, 4096)
    full = np.concatenate([img(Wd), img(Wu)], axis=1)
    return np.ascontiguousarray(full.reshape(128, 4, 2048).transpose(1, 0, 2))


def build_l0_ffn_block():
    decls = {
        "x_in": ([T, D], F32, "ExternalInput"),
        "x_out": ([T, D], F32, "ExternalOutput"),
        "L0Wc": ([8, 8, 128, 1024], F32, "ExternalInput"),
        "L0genesP": ([4, 128, 2048], F32, "ExternalInput"),
        "L0wduP": ([4, 128, 2048], F32, "ExternalInput"),
        "L0nw": ([1, 2048], F32, "ExternalInput"),
        "L0temp": ([1, 1], F32, "ExternalInput"),
        "L0scale": ([1, 1], F32, "ExternalInput"),
        "lg_spill": ([T, 4096], F32, "Internal"),
    }
    ctx = new_ctx(decls)
    emit_consts(ctx)
    xl = emit_load_x(ctx, ctx.aps["x_in"])
    ln = emit_l0_ffn(ctx, xl)
    st = emit_store_x(ctx, ctx.aps["x_out"], ln)
    ctx.p.op("gp", lambda e: None, deps=st)
    ctx.p.emit(ctx.nc)
    return ctx


def l0_ffn_shared(inputs):
    return {
        "L0Wc": pack_proj_w(np.ascontiguousarray(inputs["L0_ffn_Wc"].T), 8, 8, 2),
        "L0genesP": pack_genes(np.asarray(inputs["L0_ffn_genes"])),
        "L0wduP": pack_wdu(np.asarray(inputs["L0_ffn_Wd"]), np.asarray(inputs["L0_ffn_Wu"])),
        "L0nw": inputs["L0_ffn_norm_w"].reshape(1, 2048),
        "L0temp": np.asarray(inputs["L0_ffn_temperature"], np.float32).reshape(1, 1),
        "L0scale": np.asarray(inputs["L0_ffn_scale"], np.float32).reshape(1, 1),
    }


def run_l0_ffn_standalone(x_full, inputs):
    ctx = build_l0_ffn_block()
    hc = host_consts()
    shared = l0_ffn_shared(inputs)
    x2 = x_full.reshape(SEQ, D)
    in_maps = []
    for c in range(NCORES):
        m = dict(hc)
        m.update(core_consts(c))
        m["x_in"] = np.ascontiguousarray(x2[c * T:(c + 1) * T])
        m.update(shared)
        in_maps.append(m)
    res = run_bass_kernel_spmd(ctx.nc, in_maps, core_ids=list(range(NCORES)))
    return np.concatenate([res.results[c]["x_out"] for c in range(NCORES)], 0).reshape(1, SEQ, D)


def attn_shared(layer, inputs):
    if layer == 0:
        return {
            "L0Wqkv": pack_proj_w(np.asarray(inputs["L0_attn_Wqkv"]), 12, 8, 2),
            "L0Wout": pack_proj_w(np.asarray(inputs["L0_attn_Wout"]), 4, 8, 2),
            "L0bqkv": inputs["L0_attn_bqkv"].reshape(1, 6144),
            "L0bout": inputs["L0_attn_bout"].reshape(1, 2048),
            "L0slopes": inputs["L0_attn_slopes"].reshape(1, 16),
        }
    return {
        "L1Wqkv": pack_proj_w(np.ascontiguousarray(inputs["L1_attn_W_qkv"].T), 12, 8, 2),
        "L1Wout": pack_proj_w(np.ascontiguousarray(inputs["L1_attn_W_out"].T), 4, 8, 2),
    }


def _run_block(ctx, x_full, shared, percore_extra=None):
    hc = host_consts()
    x2 = np.ascontiguousarray(x_full.reshape(SEQ, D).astype(np.float32, copy=False))
    in_maps = []
    for c in range(NCORES):
        m = dict(hc)
        m.update(core_consts(c))
        m["x_in"] = np.ascontiguousarray(x2[c * T:(c + 1) * T])
        m.update(shared)
        if percore_extra is not None:
            m.update(percore_extra(c))
        in_maps.append(m)
    res = run_bass_kernel_spmd(ctx.nc, in_maps, core_ids=list(range(NCORES)))
    return np.concatenate([res.results[c]["x_out"] for c in range(NCORES)], 0)


_CTX_CACHE = {}


def _get(name, builder):
    if name not in _CTX_CACHE:
        _CTX_CACHE[name] = builder()
    return _CTX_CACHE[name]


def build_fused():
    decls = {
        "x_in": ([T, D], F32, "ExternalInput"),
        "x_out": ([T, D], F32, "ExternalOutput"),
        "L0Wqkv": ([12, 8, 128, 1024], F32, "ExternalInput"),
        "L0Wout": ([4, 8, 128, 1024], F32, "ExternalInput"),
        "L0bqkv": ([1, 6144], F32, "ExternalInput"),
        "L0bout": ([1, 2048], F32, "ExternalInput"),
        "L0slopes": ([1, 16], F32, "ExternalInput"),
        "L0Wc": ([8, 8, 128, 1024], F32, "ExternalInput"),
        "L0genesP": ([4, 128, 2048], F32, "ExternalInput"),
        "L0wduP": ([4, 128, 2048], F32, "ExternalInput"),
        "L0nw": ([1, 2048], F32, "ExternalInput"),
        "L0temp": ([1, 1], F32, "ExternalInput"),
        "L0scale": ([1, 1], F32, "ExternalInput"),
        "L1Wqkv": ([12, 8, 128, 1024], F32, "ExternalInput"),
        "L1Wout": ([4, 8, 128, 1024], F32, "ExternalInput"),
        "cosbf": ([1024, 128], BF16, "ExternalInput"),
        "sinbf": ([1024, 128], BF16, "ExternalInput"),
        "L1g": ([128, 128, 2048], F32, "ExternalInput"),
        "L1d": ([4, 32, 128, 1024], F32, "ExternalInput"),
        "x_spill": ([T, D], F32, "Internal"),
        "lg_spill": ([T, 4096], F32, "Internal"),
        "cc_in0": ([2064, 128], F32, "Internal"),
        "cc_out0": ([8 * 2064, 128], F32, "Internal"),
        "cc_in1": ([2064, 128], F32, "Internal"),
        "cc_out1": ([8 * 2064, 128], F32, "Internal"),
    }
    ctx = new_ctx(decls)
    emit_consts(ctx)
    xl = emit_load_x(ctx, ctx.aps["x_in"])
    ln0 = emit_attn(ctx, 0, xl, ctx.aps["x_in"], spill_first=False, cc_suffix="0")
    xt1 = []
    spl1 = []

    def cb(t, tok):
        xt1.extend(emit_build_xT(ctx, [tok] * NT, tiles=[t]))
        spl1.append(dma(ctx, "gp", ctx.aps["x_spill"][t * 128:(t + 1) * 128, :],
                        ctx.xsb[:, t * 2048:(t + 1) * 2048],
                        deps=xt1[16 * t + 14: 16 * t + 16]))
        return tok

    ln1 = emit_l0_ffn(ctx, ln0, tile_cb=cb)
    ln2 = emit_attn(ctx, 1, ln1, ctx.aps["x_spill"], spill_first=True, cc_suffix="1",
                    pre_xt=xt1, pre_spills=spl1)

    def out_cb(t, tok):
        return dma(ctx, "gp", ctx.aps["x_out"][t * 128:(t + 1) * 128, :],
                   ctx.scr_x[:, (t % 4) * 2048:((t % 4) + 1) * 2048], deps=[tok])

    st = emit_l1_ffn(ctx, ln2, out_cb)
    ctx.p.op("gp", lambda e: None, deps=st)
    ctx.p.emit(ctx.nc)
    return ctx


def kernel_fused(inputs):
    ctx = _get("fused", build_fused)
    cosb, sinb = make_rope_tables()
    shared = {}
    shared.update(attn_shared(0, inputs))
    shared.update(attn_shared(1, inputs))
    shared.update(l0_ffn_shared(inputs))
    shared["L1g"] = pack_colblocks(np.ascontiguousarray(inputs["L1_ffn_w_gate_up"].T))
    shared["L1d"] = pack_proj_w(np.ascontiguousarray(inputs["L1_ffn_w_down"].T), 4, 32, 2)
    x = _run_block(ctx, np.asarray(inputs["L0_attn_x"], np.float32), shared,
                   percore_extra=lambda c: {
                       "cosbf": np.ascontiguousarray(cosb[c * T:(c + 1) * T]),
                       "sinbf": np.ascontiguousarray(sinb[c * T:(c + 1) * T])})
    return x.reshape(1, SEQ, D)


FUSED = True


def kernel(**inputs):
    if FUSED:
        return kernel_fused(inputs)
    x = np.asarray(inputs["L0_attn_x"], np.float32)
    cosb, sinb = make_rope_tables()

    ctx = _get("a0", lambda: build_attn_block(0))
    x = _run_block(ctx, x, attn_shared(0, inputs))

    ctx = _get("f0", build_l0_ffn_block)
    x = _run_block(ctx, x, l0_ffn_shared(inputs))

    ctx = _get("a1", lambda: build_attn_block(1))
    x = _run_block(ctx, x, attn_shared(1, inputs),
                   percore_extra=lambda c: {
                       "cosbf": np.ascontiguousarray(cosb[c * T:(c + 1) * T]),
                       "sinbf": np.ascontiguousarray(sinb[c * T:(c + 1) * T])})

    ctx = _get("f1", build_l1_ffn_block)
    l1g = pack_colblocks(np.ascontiguousarray(inputs["L1_ffn_w_gate_up"].T))
    l1d = pack_proj_w(np.ascontiguousarray(inputs["L1_ffn_w_down"].T), 4, 32, 2)
    x = _run_block(ctx, x, {"L1g": l1g, "L1d": l1d})
    return x.reshape(1, SEQ, D)
